# version 02
# baseline (speedup 1.0000x reference)
import contextlib
import os
import numpy as np

import concourse.bass as bass
import concourse.mybir as mybir
from concourse.bass_utils import run_bass_kernel_spmd

F32 = mybir.dt.float32
BF16 = mybir.dt.bfloat16
U8 = mybir.dt.uint8
AO = mybir.AluOpType
AF = mybir.ActivationFunctionType
AX = mybir.AxisListType

STAGE = int(os.environ.get("K_STAGE", "100"))
L, D, DI, NS = 2048, 1024, 2048, 16
NH, DH, NE, ED = 16, 64, 8, 2048
BPC = 2
EPS = 1e-5
TC1 = 256
NC1 = L // TC1


class Plan:

    def __init__(self, nc):
        self.nc = nc
        self.ops = {"sync": [], "tensor": [], "vector": [], "scalar": [], "gpsimd": []}
        self.cnt = {}
        self.sems = {}

    def sem(self, name):
        if name not in self.sems:
            self.sems[name] = self.nc._stack.enter_context(self.nc.semaphore(name))
            self.cnt[name] = 0
        return self.sems[name]

    def on(self, eng, fn):
        self.ops[eng].append(fn)

    def inc(self, eng, fn, sem_name, amt=1):
        s = self.sem(sem_name)
        self.cnt[sem_name] += amt

        def go(e, _fn=fn, _s=s, _amt=amt):
            _fn(e).then_inc(_s, _amt)
        self.ops[eng].append(go)
        return self.cnt[sem_name]

    def inc_wait(self, eng, fn, tag="iw"):
        name = tag + "_" + eng
        v = self.inc(eng, fn, name)
        self.wait(eng, name, v)

    def selfbar(self, eng, tag="bar"):
        name = tag + "_" + eng
        v = self.inc(eng, lambda e: e.nop(), name)
        self.wait(eng, name, v)

    def wait(self, eng, sem_name, value=None):
        s = self.sem(sem_name)
        v = self.cnt[sem_name] if value is None else value
        if v <= 0:
            return

        def go(e, _s=s, _v=v):
            e.wait_ge(_s, _v)
        self.ops[eng].append(go)

    def emit(self):
        blk = self.nc._stack.enter_context(self.nc.Block())
        for eng_name in ["sync", "tensor", "vector", "scalar", "gpsimd"]:
            ops = self.ops[eng_name]
            dec = getattr(blk, eng_name)

            def make(ops_list):
                def body(eng):
                    for fn in ops_list:
                        fn(eng)
                return body
            dec(make(ops))


def cdiv(a, b):
    return (a + b - 1) // b


def align(o, a=512):
    return cdiv(o, a) * a


class Builder:
    def __init__(self, debug=False):
        self.debug = debug
        nc = bass.Bass("TRN2", target_bir_lowering=False)
        nc._stack = contextlib.ExitStack()
        self.nc = nc
        self.dbg_outs = {}

    def alloc(self):
        nc = self.nc
        st = nc._stack
        self.x_in = nc.dram_tensor("x", [BPC, L, D], F32, kind="ExternalInput").ap()
        shapes = {
            "W_in": [D, 2 * DI], "conv_w": [DI, 1, 4], "conv_b": [DI],
            "W_xproj": [DI, 2 * NS + 1], "W_dt": [1, DI], "b_dt": [DI],
            "A_log": [DI, NS], "Dp": [DI], "W_out": [DI, D],
            "wq": [D, D], "wk": [D, D], "wv": [D, D], "wo": [D, D],
            "router_w": [D, NE], "exp_fc1": [NE, D, ED], "exp_fc2": [NE, ED, D],
        }
        self.w = {n: nc.dram_tensor(n, s, F32, kind="ExternalInput").ap()
                  for n, s in shapes.items()}
        self.y_out = nc.dram_tensor("y", [BPC, L, D], F32, kind="ExternalOutput").ap()

        self.wbf1 = nc.dram_tensor("wbf1", [NE, 8, 128, ED], BF16, kind="Internal").ap()
        self.wbf2 = nc.dram_tensor("wbf2", [NE, 16, 128, D], BF16, kind="Internal").ap()
        self.h1_s = nc.dram_tensor("h1_s", [BPC, 8, 128, L], BF16, kind="Internal").ap()
        self.ypre_s = nc.dram_tensor("ypre_s", [BPC, 16, 128, L], BF16, kind="Internal").ap()
        self.gpre_s = nc.dram_tensor("gpre_s", [BPC, 16, 128, L], BF16, kind="Internal").ap()
        self.x2_s = nc.dram_tensor("x2_s", [BPC, L, D], F32, kind="Internal").ap()
        self.h2_s = nc.dram_tensor("h2_s", [BPC, L, D], BF16, kind="Internal").ap()
        self.t_s = nc.dram_tensor("t_s", [BPC, L, D], F32, kind="Internal").ap()
        self.qhat_s = nc.dram_tensor("qhat_s", [BPC, NH, 65, L], BF16, kind="Internal").ap()
        self.g_s = nc.dram_tensor("g_s", [BPC, L, D], BF16, kind="Internal").ap()

        self.arena = st.enter_context(nc.sbuf_tensor("arena", [128, 206 * 1024], U8))
        self.psum = st.enter_context(nc.psum_tensor("ps", [128, 4096], F32))
        self.pl = Plan(nc)

    def sbv(self, off, cols, dtype, rows=128):
        esz = {F32: 4, BF16: 2, U8: 1, mybir.dt.int32: 4}[dtype]
        assert off % 512 == 0 and off + cols * esz <= 206 * 1024, (off, cols)
        return self.arena.ap()[0:rows, off:off + cols * esz].bitcast(dtype)

    def bank(self, i, n=1):
        return self.psum[:, i * 512:(i + n) * 512]

    def dma(self, queue, out, in_, sem, slow=False):
        if slow:
            return self.pl.inc(queue, lambda e: e.dma_start(
                out=out, in_=in_, allow_slow_non_contiguous=True), sem, 16)
        return self.pl.inc(queue, lambda e: e.dma_start(out=out, in_=in_), sem, 16)

    def tap(self, name, ap):
        rows = int(ap.shape[0])
        cols = int(np.prod([int(d) for d in ap.shape[1:]]))
        flat = ap if len(ap.shape) == 2 else None
        assert flat is not None, "tap wants 2D"
        t = self.nc.dram_tensor("dbg_" + name, [rows, cols], ap.dtype,
                                kind="ExternalOutput").ap()
        self.dbg_outs[name] = (rows, cols)
        self.dma("gpsimd", t, flat, "dDBG")
        self.pl.wait("vector", "dDBG")
        self.pl.wait("scalar", "dDBG")
        self.pl.wait("tensor", "dDBG")
        self.pl.wait("sync", "dDBG")

    def build(self):
        self.alloc()
        self.map_constants()
        self.emit_P0()
        if STAGE >= 1:
            for b in range(BPC):
                self.emit_P1(b)
                self.emit_P2(b)
                self.emit_P3(b)
                self.emit_P4(b)
        self.emit_final()
        self.pl.emit()
        self.nc._stack.close()
        return self.nc

    def map_constants(self):
        s = self.sbv
        o = 0
        self.c_A = s(o, 16 * 16, F32); o = align(o + 16 * 16 * 4)
        self.c_Wdt = s(o, 16, F32); o = align(o + 64)
        self.c_bdt = s(o, 16, F32); o = align(o + 64)
        self.c_Dp = s(o, 16, F32); o = align(o + 64)
        self.c_convb = s(o, 16, F32); o = align(o + 64)
        self.c_convw = s(o, 16 * 4, F32); o = align(o + 256)
        self.c_xproj = s(o, 16 * 33, BF16); o = align(o + 16 * 33 * 2)
        self.c_ones = s(o, 128, F32); o = align(o + 512)
        self.c_onesb = s(o, 128, BF16); o = align(o + 256)
        self.c_ident = s(o, 128, BF16); o = align(o + 256)
        self.c_identf = s(o, 128, F32); o = align(o + 512)
        self.c_mask01 = s(o, 4 * 512, BF16); o = align(o + 4 * 512 * 2)
        self.c_router = s(o, 8 * 8, BF16); o = align(o + 128)
        self.c_onescol = s(o, 1, BF16); o = align(o + 2)
        self.c_eps = s(o, 1, F32); o = align(o + 4)
        self.c_ones2db = s(o, 128, BF16); o = align(o + 256)
        self.c_onesf64 = None
        self.OFF_P = align(o)
        assert self.OFF_P <= 16 * 1024, self.OFF_P
        self.OFF_STG = 190 * 1024

    def emit_P0(self):
        pl, w = self.pl, self.w
        self.dma("sync", self.c_A.rearrange("p (c s) -> p c s", c=16),
                 w["A_log"].rearrange("(c p) s -> p c s", p=128), "dW")
        self.dma("sync", self.c_Wdt, w["W_dt"].rearrange("o (c p) -> p (o c)", p=128), "dW", slow=True)
        self.dma("sync", self.c_bdt, w["b_dt"].rearrange("(c p) -> p c", p=128), "dW", slow=True)
        self.dma("sync", self.c_Dp, w["Dp"].rearrange("(c p) -> p c", p=128), "dW", slow=True)
        self.dma("sync", self.c_convb, w["conv_b"].rearrange("(c p) -> p c", p=128), "dW", slow=True)
        self.dma("sync", self.c_convw.rearrange("p (c k) -> p c k", c=16),
                 w["conv_w"].rearrange("(c p) o k -> p c (o k)", p=128), "dW")
        self.dma("gpsimd", self.c_xproj.rearrange("p (c n) -> p c n", c=16),
                 w["W_xproj"].rearrange("(c p) n -> p c n", p=128), "dG")
        self.dma("gpsimd", self.c_router.rearrange("p (c n) -> p c n", c=8),
                 w["router_w"].rearrange("(c p) n -> p c n", p=128), "dG")
        pl.wait("scalar", "dW")
        pl.inc("scalar", lambda e: e.activation(self.c_A, self.c_A, AF.Exp), "sC")
        pl.wait("vector", "sC")
        pl.inc("vector", lambda e: e.tensor_scalar(self.c_A, self.c_A, -1.0, None, AO.mult), "vC")

        def build_consts(g):
            g.memset(self.c_ones, 1.0)
            g.memset(self.c_onesb[0:1, :], 1.0)
            g.memset(self.c_onescol, 1.0)
            g.memset(self.c_eps, EPS)
            g.memset(self.c_ones2db, 1.0)
            g.memset(self.c_ident, 1.0)
            g.affine_select(self.c_ident, self.c_ident, pattern=[[1, 128]],
                            compare_op=AO.is_equal, fill=0.0, base=0, channel_multiplier=-1)
            g.memset(self.c_identf, 1.0)
            g.affine_select(self.c_identf, self.c_identf, pattern=[[1, 128]],
                            compare_op=AO.is_equal, fill=0.0, base=0, channel_multiplier=-1)
            m = self.c_mask01.rearrange("p (r n) -> p r n", r=4)
            g.memset(m, 1.0)
            for r in range(4):
                g.affine_select(m[:, r, :], m[:, r, :], pattern=[[1, 512]],
                                compare_op=AO.is_ge, fill=0.0, base=-128 * r,
                                channel_multiplier=-1)
            return g.nop()
        pl.inc("gpsimd", build_consts, "gC")

        stgs = [self.sbv(self.OFF_STG, 4096, BF16),
                self.sbv(self.OFF_STG + 8 * 1024, 4096, BF16)]
        k = 0
        jobs = [(self.wbf1, w["exp_fc1"], 8, ED), (self.wbf2, w["exp_fc2"], 16, D)]
        for dst, src, nch, width in jobs:
            for e in range(NE):
                for c in range(nch):
                    t = stgs[k % 2]
                    if k >= 2:
                        pl.wait("gpsimd", "dWc", (k - 1) * 16)
                    self.dma("gpsimd", t[:, :width], src[e, c * 128:(c + 1) * 128, :], "dGc")
                    pl.wait("sync", "dGc")
                    self.dma("sync", dst[e, c], t[:, :width], "dWc")
                    k += 1

    def emit_P1(self, b):
        pl = self.pl
        s = self.sbv
        o = self.OFF_P
        w_in = s(o, 8 * 2048, BF16); o = align(o + 8 * 2048 * 2)
        xt = s(o, 2 * 1024, F32); o = align(o + 2 * 1024 * 4)
        sq = s(o, 1024, F32); o = align(o + 4096)
        h1tm = s(o, 2 * 1024, BF16); o = align(o + 2 * 1024 * 2)
        h1fm = s(o, 8 * TC1, BF16); o = align(o + 8 * TC1 * 2)
        xcpre = s(o, 16 * (TC1 + 4), BF16); o = align(o + 16 * (TC1 + 4) * 2)
        halo = s(o, 16 * 4, BF16); o = align(o + 128)
        xcs = s(o, 16 * TC1, BF16); o = align(o + 16 * TC1 * 2)
        xdbl = s(o, TC1, F32, rows=33); o = align(o + TC1 * 4)
        bcflat = s(o, 32 * TC1, BF16, rows=1); o = align(o + 32 * TC1 * 2)
        brep = s(o, 16 * TC1, BF16); o = align(o + 16 * TC1 * 2)
        crep = s(o, 16 * TC1, BF16); o = align(o + 16 * TC1 * 2)
        delta = s(o, 16 * TC1, F32); o = align(o + 16 * TC1 * 4)
        dx = s(o, TC1, BF16); o = align(o + TC1 * 2)
        dA = s(o, NS * TC1, F32); o = align(o + NS * TC1 * 4)
        dxB = s(o, NS * TC1, BF16); o = align(o + NS * TC1 * 2)
        hbuf = s(o, NS * TC1, BF16); o = align(o + NS * TC1 * 2)
        hC = s(o, NS * TC1, BF16); o = align(o + NS * TC1 * 2)
        tre = s(o, 8 * TC1, BF16); o = align(o + 8 * TC1 * 2)
        hstate = s(o, 16 * 16, F32); o = align(o + 1024)
        ypre = s(o, 16 * TC1, BF16); o = align(o + 16 * TC1 * 2)
        stats = s(o, 16, F32); o = align(o + 64)
        assert o <= self.OFF_STG, o
        self.P1_END = o

        fx = "1"
        sV, sS, sT = "v" + fx, "s" + fx, "t" + fx
        self.dma("gpsimd", w_in.rearrange("p (f n) -> p f n", f=8),
                 self.w["W_in"].rearrange("(f p) n -> p f n", p=128)[:, :, 0:DI], "dG")
        pl.wait("tensor", "dG")
        pl.wait("tensor", "gC")
        pl.wait("vector", "vC")
        pl.wait("scalar", "dW")
        pl.inc("vector", lambda e: e.memset(hstate, 0.0), sV)
        pl.inc("vector", lambda e: e.memset(halo, 0.0), sV)
        for c in range(15):
            pl.on("vector", lambda e, c=c: e.memset(
                xcpre[:, c * (TC1 + 4):c * (TC1 + 4) + 4], 0.0))
        pl.inc("vector", lambda e: e.memset(
            xcpre[:, 15 * (TC1 + 4):15 * (TC1 + 4) + 4], 0.0), sV)

        A3 = self.c_A.rearrange("p (c s) -> p c s", c=16)
        cw = self.c_convw.rearrange("p (c k) -> p c k", c=16)
        silu_cnt = []

        for tc in range(NC1):
            t0 = tc * TC1
            mu = stats[:, 0:2]; ssq = stats[:, 2:4]; rr = stats[:, 4:6]; mb = stats[:, 6:8]
            if tc > 0:
                pl.wait("sync", sS)
            for j in range(2):
                xtile = xt[:, j * 1024:(j + 1) * 1024]
                self.dma("sync", xtile, self.x_in[b, t0 + j * 128:t0 + (j + 1) * 128, :], "dX")
            pl.wait("vector", "dX")
            pl.wait("scalar", "dX")
            pl.wait("scalar", sT)
            pl.on("vector", lambda e: e.reduce_sum(mu[:, 0:1], xt[:, 0:1024], AX.X))
            pl.inc_wait("vector", lambda e: e.reduce_sum(mu[:, 1:2], xt[:, 1024:2048], AX.X))
            for j in range(2):
                xtile = xt[:, j * 1024:(j + 1) * 1024]
                pl.inc("scalar", lambda e, xtile=xtile, j=j: e.activation(
                    sq, xtile, AF.Square, accum_out=ssq[:, j:j + 1]), sS)
            pl.inc_wait("vector", lambda e: e.tensor_scalar(mu, mu, 1.0 / D, None, AO.mult))
            pl.wait("vector", sS)
            pl.inc_wait("vector", lambda e: e.tensor_scalar(ssq, ssq, 1.0 / D, EPS, AO.mult, op1=AO.add))
            pl.inc_wait("vector", lambda e: e.tensor_tensor(rr, mu, mu, AO.mult))
            pl.inc("vector", lambda e: e.tensor_tensor(ssq, ssq, rr, AO.subtract), sV)
            pl.wait("scalar", sV)
            pl.inc_wait("scalar", lambda e: e.activation(ssq, ssq, AF.Ln))
            pl.inc("scalar", lambda e: e.activation(rr, ssq, AF.Exp, scale=-0.5), sS)
            pl.wait("vector", sS)
            pl.inc_wait("vector", lambda e: e.tensor_tensor(mb, mu, rr, AO.mult))
            pl.inc("vector", lambda e: e.tensor_scalar(mb, mb, -1.0, None, AO.mult), sV)
            pl.wait("scalar", sV)
            for j in range(2):
                xtile = xt[:, j * 1024:(j + 1) * 1024]
                pl.inc("scalar", lambda e, xtile=xtile, j=j: e.activation(
                    h1tm[:, j * 1024:(j + 1) * 1024], xtile, AF.Identity,
                    scale=rr[:, j:j + 1], bias=mb[:, j:j + 1]), sS)
            if self.debug and b == 0 and tc == 0 and STAGE >= 7:
                pl.wait("gpsimd", sS)
                self.tap("p1_xt", xt)
                self.tap("p1_stats", stats)
                self.tap("p1_h1tm", h1tm)
            if STAGE < 2:
                continue
            pl.wait("tensor", sS)
            pl.wait("tensor", sV)
            pl.wait("tensor", "sbc" + fx)
            pl.wait("scalar", "dS")
            for j in range(2):
                for c in range(8):
                    pb = self.bank(6 + (j * 8 + c) % 2)[:, 0:64].bitcast(BF16)
                    if j * 8 + c >= 2:
                        pl.wait("tensor", sS)
                    pl.inc("tensor", lambda e, j=j, c=c, pb=pb: e.transpose(
                        pb, h1tm[:, j * 1024 + c * 128:j * 1024 + (c + 1) * 128],
                        self.c_ident), sT)
                    pl.wait("scalar", sT)
                    pl.inc("scalar", lambda e, j=j, c=c, pb=pb: e.copy(
                        h1fm[:, c * TC1 + j * 128:c * TC1 + (j + 1) * 128], pb), sS)
            pl.wait("gpsimd", sS)
            self.dma("gpsimd", self.h1_s[b, :, :, t0:t0 + TC1].rearrange("c p t -> p c t"),
                     h1fm.rearrange("p (c t) -> p c t", c=8), "dS")
            if STAGE < 3:
                continue
            pl.wait("tensor", sS)
            for pn in range(16):
                pb = self.bank(pn % 4)[:, 0:TC1]
                if pn >= 4:
                    pl.wait("tensor", sS)
                for f in range(8):
                    pl.inc("tensor", lambda e, pn=pn, f=f, pb=pb: e.matmul(
                        pb, w_in.rearrange("p (f n) -> p f n", f=8)[:, f, pn * 128:(pn + 1) * 128],
                        h1fm[:, f * TC1:(f + 1) * TC1], start=(f == 0), stop=(f == 7)), sT)
                pl.wait("scalar", sT)
                pl.inc("scalar", lambda e, pn=pn, pb=pb: e.copy(
                    xcpre[:, pn * (TC1 + 4) + 4:(pn + 1) * (TC1 + 4)], pb), sS)
            if STAGE < 4:
                continue
            pl.wait("vector", sS)
            for c in range(16):
                base = c * (TC1 + 4)
                acc = sq[:, (c % 2) * TC1:(c % 2 + 1) * TC1]
                if len(silu_cnt) >= 2:
                    pl.wait("vector", "sil" + fx, silu_cnt[-2])
                pl.on("vector", lambda e, base=base, c=c, acc=acc: e.tensor_scalar(
                    acc, xcpre[:, base + 1:base + 1 + TC1], cw[:, c, 0:1], None, AO.mult))
                for k in range(1, 3):
                    pl.on("vector", lambda e, base=base, c=c, k=k, acc=acc: e.scalar_tensor_tensor(
                        acc, xcpre[:, base + 1 + k:base + 1 + k + TC1], cw[:, c, k:k + 1], acc,
                        AO.mult, AO.add))
                pl.inc("vector", lambda e, base=base, c=c, acc=acc: e.scalar_tensor_tensor(
                    acc, xcpre[:, base + 4:base + 4 + TC1], cw[:, c, 3:4], acc,
                    AO.mult, AO.add), sV)
                pl.wait("scalar", sV)
                n = pl.inc("scalar", lambda e, c=c, acc=acc: e.activation(
                    xcs[:, c * TC1:(c + 1) * TC1], acc, AF.Silu,
                    bias=self.c_convb[:, c:c + 1]), "sil" + fx)
                silu_cnt.append(n)
                if c < 15:
                    pl.on("vector", lambda e, c=c, base=base: e.tensor_copy(
                        halo[:, c * 4:(c + 1) * 4], xcpre[:, base + TC1:base + TC1 + 4]))
                else:
                    pl.inc("vector", lambda e, c=c, base=base: e.tensor_copy(
                        halo[:, c * 4:(c + 1) * 4], xcpre[:, base + TC1:base + TC1 + 4]), sV)
            for c in range(16):
                if c < 15:
                    pl.on("vector", lambda e, c=c: e.tensor_copy(
                        xcpre[:, c * (TC1 + 4):c * (TC1 + 4) + 4], halo[:, c * 4:(c + 1) * 4]))
                else:
                    pl.inc("vector", lambda e, c=c: e.tensor_copy(
                        xcpre[:, c * (TC1 + 4):c * (TC1 + 4) + 4], halo[:, c * 4:(c + 1) * 4]), sV)
            if STAGE < 5:
                continue
            pl.wait("tensor", "sil" + fx)
            pxd = self.bank(5)[0:33, 0:TC1]
            xp3 = self.c_xproj.rearrange("p (c n) -> p c n", c=16)
            for c in range(16):
                pl.inc("tensor", lambda e, c=c, pxd=pxd: e.matmul(
                    pxd, xp3[:, c, :], xcs[:, c * TC1:(c + 1) * TC1],
                    start=(c == 0), stop=(c == 15)), sT)
            pl.wait("scalar", sT)
            pl.wait("scalar", "dB")
            pl.inc("scalar", lambda e, pxd=pxd: e.copy(xdbl, pxd), sS)
            pl.wait("tensor", sS)
            pd0 = self.bank(6)[:, 0:TC1]
            pl.inc("tensor", lambda e, pd0=pd0: e.matmul(
                pd0, self.c_ones[0:1, :], xdbl[0:1, :], start=True, stop=True), sT)
            pl.wait("vector", sT)
            for c in range(16):
                pl.on("vector", lambda e, c=c, pd0=pd0: e.tensor_scalar(
                    delta[:, c * TC1:(c + 1) * TC1], pd0, self.c_Wdt[:, c:c + 1],
                    self.c_bdt[:, c:c + 1], AO.mult, op1=AO.add))
            pl.inc("vector", lambda e: e.tensor_scalar(delta, delta, 80.0, None, AO.min), sV)
            pl.wait("scalar", sV)
            pl.on("scalar", lambda e: e.activation(delta, delta, AF.Exp))
            pl.inc("scalar", lambda e: e.activation(delta, delta, AF.Ln, bias=1.0, scale=1.0), "sd" + fx)
            if STAGE < 6:
                continue
            pl.wait("gpsimd", sS)
            pl.wait("gpsimd", sT)
            self.dma("gpsimd", bcflat[0:1, :], xdbl[1:33, :], "dB")
            pl.wait("tensor", "dB")
            pl.wait("tensor", sV)
            nrep = NS * TC1
            for half, rep in ((0, brep), (1, crep)):
                for piece in range(nrep // 512):
                    pb = self.bank(6 + piece % 2)[:, 0:512]
                    src = bcflat[0:1, half * nrep + piece * 512:half * nrep + (piece + 1) * 512]
                    if piece >= 2 or half > 0:
                        pl.wait("tensor", "sbc" + fx)
                    pl.inc("tensor", lambda e, pb=pb, src=src: e.matmul(
                        pb, self.c_onesb[0:1, :], src, start=True, stop=True), sT)
                    pl.wait("scalar", sT)
                    pl.inc("scalar", lambda e, rep=rep, piece=piece, pb=pb: e.copy(
                        rep[:, piece * 512:(piece + 1) * 512], pb), "sbc" + fx)
            if STAGE < 7:
                continue
            pl.wait("vector", "dS")
            pl.wait("vector", "sd" + fx)
            pl.wait("vector", "sbc" + fx)
            pl.wait("vector", "sil" + fx)
            da3 = dA.rearrange("p (s t) -> p s t", s=NS)
            dxB3 = dxB.rearrange("p (s t) -> p s t", s=NS)
            br3 = brep.rearrange("p (s t) -> p s t", s=NS)
            cr3 = crep.rearrange("p (s t) -> p s t", s=NS)
            h3 = hbuf.rearrange("p (s t) -> p s t", s=NS)
            hC3 = hC.rearrange("p (s t) -> p s t", s=NS)
            t3 = tre.rearrange("p (s t) -> p s t", s=8)
            hst = hstate.rearrange("p (c s) -> p c s", c=16)
            tmp16 = sq[:, 512:512 + NS]
            for c in range(16):
                if c > 0:
                    pl.wait("vector", "sda" + fx)
                for si in range(NS - 1):
                    pl.on("vector", lambda e, c=c, si=si: e.tensor_scalar(
                        da3[:, si, :], delta[:, c * TC1:(c + 1) * TC1],
                        A3[:, c, si:si + 1], None, AO.mult))
                pl.inc("vector", lambda e, c=c: e.tensor_scalar(
                    da3[:, NS - 1, :], delta[:, c * TC1:(c + 1) * TC1],
                    A3[:, c, NS - 1:NS], None, AO.mult), "vda" + fx)
                pl.wait("scalar", "vda" + fx)
                pl.inc("scalar", lambda e: e.activation(dA, dA, AF.Exp), "sda" + fx)
                pl.on("vector", lambda e, c=c: e.tensor_tensor(
                    dx, delta[:, c * TC1:(c + 1) * TC1],
                    xcs[:, c * TC1:(c + 1) * TC1], AO.mult))
                dxv = dx.unsqueeze(1).broadcast_to((128, NS, TC1))
                pl.on("vector", lambda e, dxv=dxv: e.tensor_tensor(dxB3, dxv, br3, AO.mult))
                pl.wait("vector", "sda" + fx)
                pl.inc_wait("vector", lambda e, c=c: e.tensor_tensor(
                    tmp16, da3[:, :, 0], hst[:, c, :], AO.mult))
                pl.inc_wait("vector", lambda e: e.tensor_tensor(
                    dxB3[:, :, 0], dxB3[:, :, 0], tmp16, AO.add))
                pl.inc_wait("vector", lambda e: e.memset(da3[:, :, 0], 0.0))
                pl.inc_wait("vector", lambda e: e.tensor_tensor_scan(
                    hbuf, dA, dxB, 0.0, AO.mult, AO.add))
                pl.inc_wait("vector", lambda e, c=c: e.tensor_copy(hst[:, c, :], h3[:, :, TC1 - 1]))
                pl.on("vector", lambda e: e.tensor_tensor(hC3, h3, cr3, AO.mult))
                pl.on("vector", lambda e: e.tensor_tensor(t3, hC3[:, 0:8, :], hC3[:, 8:16, :], AO.add))
                pl.on("vector", lambda e: e.tensor_tensor(t3[:, 0:4, :], t3[:, 0:4, :], t3[:, 4:8, :], AO.add))
                pl.on("vector", lambda e: e.tensor_tensor(t3[:, 0:2, :], t3[:, 0:2, :], t3[:, 2:4, :], AO.add))
                pl.on("vector", lambda e: e.tensor_tensor(t3[:, 0:1, :], t3[:, 0:1, :], t3[:, 1:2, :], AO.add))
                pl.inc("vector", lambda e, c=c: e.scalar_tensor_tensor(
                    ypre[:, c * TC1:(c + 1) * TC1], xcs[:, c * TC1:(c + 1) * TC1],
                    self.c_Dp[:, c:c + 1], t3[:, 0, :], AO.mult, AO.add), sV)
            pl.wait("gpsimd", sV)
            self.dma("gpsimd", self.ypre_s[b, :, :, t0:t0 + TC1].rearrange("c p t -> p c t"),
                     ypre.rearrange("p (c t) -> p c t", c=16), "dS")
            if self.debug and b == 0 and tc < 2 and STAGE >= 7:
                self.tap(f"p1_ypre_tc{tc}", ypre)
                if tc == 0:
                    self.tap("p1_xcpre", xcpre)
                    self.tap("p1_winsb", w_in)
                    self.tap("p1_xproj_const", self.c_xproj)
                    self.tap("p1_xdbl", xdbl)
                    self.tap("p1_h1fm", h1fm)
                    self.tap("p1_xcs", xcs)
                    self.tap("p1_delta", delta)
                    self.tap("p1_brep", brep)
                    self.tap("p1_h", hbuf)


    def emit_P2(self, b):
        if STAGE < 80:
            return
        pl = self.pl
        s = self.sbv
        O = self.OFF_P
        fx = "2"
        sV, sS, sT = "v" + fx, "s" + fx, "t" + fx

        za_h1 = s(O, 8 * 2048, BF16)
        za_wz = s(O + 32 * 1024, 8 * 2048, BF16)
        zr_yp = s(O + 64 * 1024, 2 * 2048, BF16)
        zr_gp = s(O + 80 * 1024, 2 * 2048, BF16)
        pl.wait("gpsimd", "v1")
        pl.wait("gpsimd", "s1")
        pl.wait("gpsimd", "t1")
        pl.wait("sync", "v1")
        pl.wait("sync", "t1")
        self.dma("gpsimd", za_wz.rearrange("p (f n) -> p f n", f=8),
                 self.w["W_in"].rearrange("(f p) n -> p f n", p=128)[:, :, DI:], "dG")
        pl.wait("sync", "dS")
        self.dma("sync", za_h1.rearrange("p (f n) -> p f n", f=8),
                 self.h1_s[b].rearrange("f p t -> p f t"), "dA" + fx)
        pl.wait("tensor", "dG")
        pl.wait("tensor", "dA" + fx)
        h13 = za_h1.rearrange("p (f n) -> p f n", f=8)
        wz3 = za_wz.rearrange("p (f n) -> p f n", f=8)
        for zp in range(16):
            pb = self.bank(4 * (zp % 2), 4)
            if zp >= 2:
                pl.wait("tensor", sS)
            yb = zr_yp[:, (zp % 2) * 2048:(zp % 2 + 1) * 2048]
            if zp >= 2:
                pl.wait("sync", sV)
            self.dma("sync", yb.rearrange("p n -> p n"), self.ypre_s[b, zp], "dY" + fx)
            for f in range(8):
                for pc in range(4):
                    pl.inc("tensor", lambda e, zp=zp, f=f, pc=pc, pb=pb: e.matmul(
                        pb[:, pc * 512:(pc + 1) * 512],
                        wz3[:, f, zp * 128:(zp + 1) * 128],
                        h13[:, f, pc * 512:(pc + 1) * 512],
                        start=(f == 0), stop=(f == 7)), sT)
            pl.wait("scalar", sT)
            gb = zr_gp[:, (zp % 2) * 2048:(zp % 2 + 1) * 2048]
            if zp >= 2:
                pl.wait("scalar", "dZ" + fx)
            pl.inc("scalar", lambda e, pb=pb, gb=gb: e.activation(gb, pb, AF.Silu), sS)
            pl.wait("vector", sS)
            pl.wait("vector", "dY" + fx)
            pl.inc("vector", lambda e, gb=gb, yb=yb: e.tensor_tensor(gb, gb, yb, AO.mult), sV)
            pl.wait("gpsimd", sV)
            self.dma("gpsimd", self.gpre_s[b, zp], gb, "dZ" + fx)

        if STAGE < 85:
            return
        wb_wout = s(O, 16 * 1024, BF16)
        wb_gs = s(O + 32 * 1024, 2 * 16 * 128, BF16)
        wb_x = s(O + 40 * 1024, 2 * 1024, F32)
        wb_x2 = s(O + 48 * 1024, 2 * 1024, F32)
        wb_h2 = s(O + 56 * 1024, 2 * 1024, BF16)
        wb_rmst = s(O + 60 * 1024, 2 * 1024, BF16)
        wb_sq = s(O + 64 * 1024, 1024, F32)
        wb_st = s(O + 68 * 1024, 16, F32)
        rms_fm = s(O + 72 * 1024, 8 * 2048, BF16)
        pl.wait("gpsimd", sT)
        pl.wait("gpsimd", sV)
        pl.wait("sync", sT)
        pl.wait("sync", sS)
        self.dma("gpsimd", wb_wout.rearrange("p (c n) -> p c n", c=16),
                 self.w["W_out"].rearrange("(c p) n -> p c n", p=128), "dG")
        pl.wait("tensor", "dG")
        pl.wait("sync", "dZ" + fx)
        wo3 = wb_wout.rearrange("p (c n) -> p c n", c=16)
        st_ = wb_st
        for tp in range(16):
            par = tp % 2
            pb = self.bank(2 * par, 2)
            gsb = wb_gs[:, par * 2048:(par + 1) * 2048].rearrange("p (c n) -> p c n", c=16)
            xb = wb_x[:, par * 1024:(par + 1) * 1024]
            x2b = wb_x2[:, par * 1024:(par + 1) * 1024]
            h2b = wb_h2[:, par * 1024:(par + 1) * 1024]
            rmb = wb_rmst[:, par * 1024:(par + 1) * 1024]
            if tp >= 2:
                pl.wait("sync", sT)
                pl.wait("sync", sV)
            self.dma("sync", gsb, self.gpre_s[b, :, :, tp * 128:(tp + 1) * 128].rearrange("c p t -> p c t"), "dA" + fx)
            self.dma("sync", xb, self.x_in[b, tp * 128:(tp + 1) * 128, :], "dA" + fx)
            pl.wait("tensor", "dA" + fx)
            pl.wait("tensor", "dZ" + fx)
            if tp >= 2:
                pl.wait("tensor", sS)
            for dc in range(16):
                for pc in range(2):
                    pl.inc("tensor", lambda e, dc=dc, pc=pc, pb=pb, gsb=gsb: e.matmul(
                        pb[:, pc * 512:(pc + 1) * 512], gsb[:, dc, :],
                        wo3[:, dc, pc * 512:(pc + 1) * 512],
                        start=(dc == 0), stop=(dc == 15)), sT)
            pl.wait("vector", sT)
            pl.wait("scalar", sT)
            if STAGE < 86:
                pl.inc("scalar", lambda e, pb=pb: e.copy(wb_sq, pb), sS)
                continue
            mu = st_[:, 0:1]; ssq = st_[:, 1:2]; rr = st_[:, 2:3]; mb2 = st_[:, 3:4]
            pl.inc_wait("vector", lambda e, pb=pb, mu=mu: e.reduce_sum(mu, pb, AX.X))
            pl.wait("scalar", "iw_vector")
            pl.inc("scalar", lambda e, pb=pb, ssq=ssq: e.activation(
                wb_sq, pb, AF.Square, accum_out=ssq), sS)
            pl.inc_wait("vector", lambda e, mu=mu: e.tensor_scalar(mu, mu, 1.0 / D, None, AO.mult))
            pl.wait("vector", sS)
            pl.inc_wait("vector", lambda e, ssq=ssq: e.tensor_scalar(ssq, ssq, 1.0 / D, EPS, AO.mult, op1=AO.add))
            pl.inc_wait("vector", lambda e, rr=rr, mu=mu: e.tensor_tensor(rr, mu, mu, AO.mult))
            pl.inc("vector", lambda e, ssq=ssq, rr=rr: e.tensor_tensor(ssq, ssq, rr, AO.subtract), sV)
            pl.wait("scalar", sV)
            pl.inc_wait("scalar", lambda e, ssq=ssq: e.activation(ssq, ssq, AF.Ln))
            pl.inc("scalar", lambda e, ssq=ssq, rr=rr: e.activation(rr, ssq, AF.Exp, scale=-0.5), sS)
            pl.wait("vector", sS)
            pl.inc_wait("vector", lambda e, mb2=mb2, mu=mu, rr=rr: e.tensor_tensor(mb2, mu, rr, AO.mult))
            pl.inc("vector", lambda e, mb2=mb2: e.tensor_scalar(mb2, mb2, -1.0, None, AO.mult), sV)
            pl.wait("scalar", sV)
            pl.inc("scalar", lambda e, pb=pb, rr=rr, mb2=mb2: e.activation(
                wb_sq, pb, AF.Identity, scale=rr, bias=mb2), sS)
            pl.wait("vector", sS)
            pl.wait("vector", "dA" + fx)
            if tp >= 2:
                pl.wait("vector", "dX2" + fx)
            pl.inc("vector", lambda e, x2b=x2b, xb=xb: e.tensor_tensor(x2b, wb_sq, xb, AO.add), sV)
            pl.wait("gpsimd", sV)
            self.dma("gpsimd", self.x2_s[b, tp * 128:(tp + 1) * 128, :], x2b, "dX2" + fx)
            pl.inc_wait("vector", lambda e, x2b=x2b, mu=mu: e.reduce_sum(mu, x2b, AX.X))
            pl.wait("scalar", sV)
            pl.wait("scalar", "iw_vector")
            pl.inc("scalar", lambda e, x2b=x2b, ssq=ssq: e.activation(
                wb_sq, x2b, AF.Square, accum_out=ssq), sS)
            pl.inc_wait("vector", lambda e, mu=mu: e.tensor_scalar(mu, mu, 1.0 / D, None, AO.mult))
            pl.wait("vector", sS)
            pl.inc_wait("vector", lambda e, ssq=ssq: e.tensor_scalar(ssq, ssq, 1.0 / D, EPS, AO.mult, op1=AO.add))
            pl.inc_wait("vector", lambda e, rr=rr, mu=mu: e.tensor_tensor(rr, mu, mu, AO.mult))
            pl.inc("vector", lambda e, ssq=ssq, rr=rr: e.tensor_tensor(ssq, ssq, rr, AO.subtract), sV)
            pl.wait("scalar", sV)
            pl.inc_wait("scalar", lambda e, ssq=ssq: e.activation(ssq, ssq, AF.Ln))
            pl.inc("scalar", lambda e, ssq=ssq, rr=rr: e.activation(rr, ssq, AF.Exp, scale=-0.5), sS)
            pl.wait("vector", sS)
            pl.inc_wait("vector", lambda e, mb2=mb2, mu=mu, rr=rr: e.tensor_tensor(mb2, mu, rr, AO.mult))
            pl.inc("vector", lambda e, mb2=mb2: e.tensor_scalar(mb2, mb2, -1.0, None, AO.mult), sV)
            pl.wait("scalar", sV)
            if tp >= 2:
                pl.wait("scalar", "dH2" + fx)
            pl.inc("scalar", lambda e, h2b=h2b, x2b=x2b, rr=rr, mb2=mb2: e.activation(
                h2b, x2b, AF.Identity, scale=rr, bias=mb2), sS)
            pl.wait("gpsimd", sS)
            self.dma("gpsimd", self.h2_s[b, tp * 128:(tp + 1) * 128, :], h2b, "dH2" + fx)
            if self.debug and b == 0 and tp < 2:
                pl.wait("gpsimd", sV)
                pl.wait("gpsimd", sS)
                self.tap(f"p2_x2_tp{tp}", x2b)
                self.tap(f"p2_h2_tp{tp}", h2b)
            pl.inc("scalar", lambda e, h2b=h2b, ssq=ssq: e.activation(
                wb_sq, h2b, AF.Square, accum_out=ssq), sS)
            pl.wait("scalar", sS)
            pl.inc_wait("scalar", lambda e, ssq=ssq: e.activation(ssq, ssq, AF.Ln, scale=1.0 / D, bias=self.c_eps))
            pl.inc("scalar", lambda e, ssq=ssq, rr=rr: e.activation(rr, ssq, AF.Exp, scale=-0.5), sS)
            pl.wait("scalar", sS)
            pl.inc("scalar", lambda e, rmb=rmb, h2b=h2b, rr=rr: e.activation(
                rmb, h2b, AF.Identity, scale=rr, bias=0.0), sS)
            if STAGE < 87:
                continue
            pl.wait("tensor", sS)
            for c in range(8):
                pbt = self.bank(4 + c % 2)[:, 0:64].bitcast(BF16)
                if c >= 2 or tp > 0:
                    pl.wait("tensor", sS)
                pl.inc("tensor", lambda e, c=c, pbt=pbt, rmb=rmb: e.transpose(
                    pbt, rmb[:, c * 128:(c + 1) * 128], self.c_ident), sT)
                pl.wait("scalar", sT)
                pl.inc("scalar", lambda e, c=c, pbt=pbt, tp=tp: e.copy(
                    rms_fm[:, c * 2048 + tp * 128: c * 2048 + (tp + 1) * 128], pbt), sS)

        if STAGE < 88:
            return
        vhat = s(O, 16 * 16 * 65, BF16)
        wv_sb = s(O + 34 * 1024, 8 * 1024, BF16)
        qk_pool = s(O + 50 * 1024, 2 * 8 * 64, BF16)
        qstag = s(O + 52 * 1024, 2 * 512, BF16, rows=65)
        pl.wait("gpsimd", sT)
        pl.wait("gpsimd", sV)
        pl.wait("gpsimd", sS)
        self.dma("gpsimd", wv_sb.rearrange("p (f n) -> p f n", f=8),
                 self.w["wv"].rearrange("(f p) n -> p f n", p=128), "dG")
        pl.wait("tensor", "dG")
        v4 = vhat.rearrange("p (h k n) -> p h k n", h=16, k=16)
        for tp in range(16):
            pb = self.bank(2 * (tp % 2), 2)
            if tp >= 2:
                pl.wait("tensor", sS)
            for f in range(8):
                for pc in range(2):
                    pl.inc("tensor", lambda e, f=f, pc=pc, pb=pb, tp=tp: e.matmul(
                        pb[:, pc * 512:(pc + 1) * 512],
                        rms_fm[:, f * 2048 + tp * 128: f * 2048 + (tp + 1) * 128],
                        wv_sb.rearrange("p (f n) -> p f n", f=8)[:, f, pc * 512:(pc + 1) * 512],
                        start=(f == 0), stop=(f == 7)), sT)
            pl.wait("scalar", sT)
            pl.inc("scalar", lambda e, pb=pb, tp=tp: e.copy(
                v4[:, :, tp, 0:64], pb.rearrange("p (h n) -> p h n", h=16)), sS)
        pl.wait("gpsimd", sS)
        pl.inc("gpsimd", lambda e: e.memset(v4[:, :, :, 64:65], 1.0), "gV" + fx)
        khat = s(O + 104 * 1024, 16 * 2048, BF16)
        for mat, dst in (("wq", "Q"), ("wk", "K")):
            for h in range(NH):
                pool = qk_pool[:, (h % 2) * 512:(h % 2) * 512 + 512].rearrange(
                    "p (f n) -> p f n", f=8)
                if h >= 2:
                    pl.wait("gpsimd", sT)
                self.dma("gpsimd", pool,
                         self.w[mat].rearrange("(f p) n -> p f n", p=128)[:, :, h * 64:(h + 1) * 64],
                         "dQ" + fx)
                pl.wait("tensor", "dQ" + fx)
                pl.wait("tensor", sS)
                pl.wait("tensor", "sq" + fx)
                for tcq in range(4):
                    pbq = self.bank(4 + tcq)[0:64, :]
                    for f in range(8):
                        pl.inc("tensor", lambda e, f=f, tcq=tcq, pool=pool, pbq=pbq: e.matmul(
                            pbq, pool[:, f, :],
                            rms_fm[:, f * 2048 + tcq * 512: f * 2048 + (tcq + 1) * 512],
                            start=(f == 0), stop=(f == 7)), sT)
                pl.wait("scalar", sT)
                for tcq in range(4):
                    pbq = self.bank(4 + tcq)[0:64, :]
                    if dst == "Q":
                        qs = qstag[:, (tcq % 2) * 512:(tcq % 2 + 1) * 512]
                        if tcq >= 2:
                            pl.wait("scalar", "dQs" + fx)
                        pl.inc("scalar", lambda e, pbq=pbq, qs=qs: e.activation(
                            qs[0:64, :], pbq, AF.Identity, scale=0.125, bias=0.0), "sq" + fx)
                        pl.wait("gpsimd", "sq" + fx)
                        self.dma("gpsimd", self.qhat_s[b, h, 0:64, tcq * 512:(tcq + 1) * 512],
                                 qs[0:64, :], "dQs" + fx)
                    else:
                        pl.inc("scalar", lambda e, pbq=pbq, h=h, tcq=tcq: e.copy(
                            khat[0:64, h * 2048 + tcq * 512: h * 2048 + (tcq + 1) * 512], pbq), sS)
        pl.wait("gpsimd", sS)
        pl.inc("gpsimd", lambda e: e.memset(
            khat.rearrange("p (h n) -> p h n", h=16)[64:65, :, :], 1.0), "gV" + fx)
        if self.debug and b == 0:
            pl.wait("gpsimd", sS)
            self.tap("p2_rmsfm", rms_fm)
            self.tap("p2_khat", khat)
            self.tap("p2_vhat", vhat)

    def emit_P3(self, b):
        if STAGE < 90:
            return
        pl = self.pl
        s = self.sbv
        O = self.OFF_P
        fx = "3"
        sV, sS, sT = "v" + fx, "s" + fx, "t" + fx
        vhat = s(O, 16 * 16 * 65, BF16)
        v4 = vhat.rearrange("p (h k n) -> p h k n", h=16, k=16)
        khat = s(O + 104 * 1024, 16 * 2048, BF16)
        wo_sb = s(O + 34 * 1024, 8 * 1024, BF16)
        qsb = s(O + 50 * 1024, 2 * 512, BF16, rows=65)
        qsq = s(O + 52 * 1024, 512, BF16, rows=64)
        lnk_all = s(O + 53 * 1024, 16, F32)
        ptil = s(O + 54 * 1024, 2 * 1024, BF16)
        o_all = s(O + 58 * 1024, 8 * 512, BF16)
        h2sb = s(O + 66 * 1024, 2 * 1024, F32)
        tsb = s(O + 74 * 1024, 2 * 1024, F32)
        gtm = s(O + 82 * 1024, 2 * 1024, BF16)
        p3sq = s(O + 86 * 1024, 1024, F32)
        p3st = s(O + 90 * 1024, 16, F32)
        ksq = s(O + 91 * 1024, 2048, BF16, rows=64)
        rl_sb = s(O + 95 * 1024, 512, F32)
        lb_sb = s(O + 96 * 1024, 512, BF16, rows=128)
        rtm = s(O + 98 * 1024, 16, F32)
        pl.wait("gpsimd", "t2")
        pl.wait("gpsimd", "s2")
        self.dma("gpsimd", wo_sb.rearrange("p (c n) -> p c n", c=8),
                 self.w["wo"].rearrange("(c p) n -> p c n", p=128), "dG")
        pl.wait("sync", "dQs2")

        for h in range(NH):
            pl.inc("scalar", lambda e, h=h: e.activation(
                ksq, khat[0:64, h * 2048:(h + 1) * 2048], AF.Square), "sk" + fx)
            pl.wait("tensor", "sk" + fx)
            pkn = self.bank(4, 4)[0:1, :]
            if h > 0:
                pl.wait("tensor", sV)
            for pc in range(4):
                pl.inc("tensor", lambda e, pc=pc, pkn=pkn: e.matmul(
                    pkn[:, pc * 512:(pc + 1) * 512], self.c_onescol[0:64, :],
                    ksq[:, pc * 512:(pc + 1) * 512], start=True, stop=True), sT)
            pl.wait("vector", sT)
            pl.inc_wait("vector", lambda e, pkn=pkn: e.reduce_max(rtm[0:1, 0:1], pkn, AX.X))
            pl.inc("vector", lambda e, h=h: e.tensor_copy(rtm[0:1, h + 0:h + 1], rtm[0:1, 0:1]), sV) \
                if False else pl.inc("vector", lambda e, h=h: e.tensor_copy(
                    lnk_all[0:1, h:h + 1], rtm[0:1, 0:1]), sV)
        pl.wait("scalar", sV)
        pl.inc("scalar", lambda e: e.activation(
            lnk_all[0:1, :], lnk_all[0:1, :], AF.Ln), "sk" + fx)
        pl.wait("tensor", "sk" + fx)
        pkb = self.bank(5)[:, 0:16]
        pl.inc("tensor", lambda e, pkb=pkb: e.matmul(
            pkb, self.c_ones[0:1, :], lnk_all[0:1, :], start=True, stop=True), sT)
        pl.wait("scalar", sT)
        pl.inc_wait("scalar", lambda e, pkb=pkb: e.activation(
            lnk_all, pkb, AF.Identity, scale=0.5, bias=0.0))

        for qc in range(4):
            oc3 = o_all.rearrange("p (c n) -> p c n", c=8)
            for h in range(NH):
                par = (qc * NH + h) % 2
                qt = qsb[:, par * 512:(par + 1) * 512]
                if qc * NH + h >= 2:
                    pl.wait("sync", sT)
                self.dma("sync", qt[0:64, :], self.qhat_s[b, h, 0:64, qc * 512:(qc + 1) * 512],
                         "dQ" + fx)
                pl.wait("scalar", "dQ" + fx)
                pl.inc("scalar", lambda e, qt=qt: e.activation(qsq, qt[0:64, :], AF.Square), "sq" + fx)
                pl.wait("tensor", "sq" + fx)
                pM = self.bank(5)[64:65, 0:512]
                if qc * NH + h > 0:
                    pl.wait("tensor", "sm" + fx)
                pl.inc("tensor", lambda e, pM=pM: e.matmul(
                    pM, self.c_onescol[0:64, :], qsq, start=True, stop=True), sT)
                pl.wait("scalar", sT)
                pl.inc_wait("scalar", lambda e, pM=pM: e.activation(pM, pM, AF.Ln))
                pl.inc("scalar", lambda e, pM=pM, qt=qt, h=h: e.activation(
                    qt[64:65, :], pM, AF.Exp, scale=0.5,
                    bias=lnk_all[64:65, h:h + 1]), "sm" + fx)
                pl.wait("vector", "sm" + fx)
                pl.inc("vector", lambda e, qt=qt: e.tensor_scalar(
                    qt[64:65, :], qt[64:65, :], -1.0, None, AO.mult), "vm" + fx)
                pl.wait("tensor", "vm" + fx)
                if qc * NH + h > 0:
                    pl.wait("tensor", "so" + fx)
                pl.wait("tensor", sV)
                nkc = 4 * (qc + 1)
                m3 = self.c_mask01.rearrange("p (r n) -> p r n", r=4)
                pO = self.bank(4)[0:65, :]
                for kp in range(nkc // 2):
                    pg = kp % 2
                    if kp >= 2:
                        pl.wait("tensor", "vav" + fx)
                    for i in range(2):
                        kc = 2 * kp + i
                        pS = self.bank(2 * pg + i)
                        pl.inc("tensor", lambda e, pS=pS, h=h, kc=kc, qt=qt: e.matmul(
                            pS, khat[0:65, h * 2048 + kc * 128: h * 2048 + (kc + 1) * 128],
                            qt, start=True, stop=True), sT)
                    pl.wait("scalar", sT)
                    pl.inc("scalar", lambda e, pg=pg: e.activation(
                        ptil[:, pg * 1024:(pg + 1) * 1024],
                        self.bank(2 * pg, 2), AF.Exp), "se" + fx)
                    if 2 * kp >= 4 * qc:
                        pl.wait("vector", "se" + fx)
                        for i in range(2):
                            kc = 2 * kp + i
                            r = kc - 4 * qc
                            sl = ptil[:, pg * 1024 + i * 512: pg * 1024 + (i + 1) * 512]
                            pl.inc("vector", lambda e, sl=sl, r=r: e.tensor_tensor(
                                sl, sl, m3[:, r, :], AO.mult), "se" + fx)
                    pl.wait("tensor", "se" + fx)
                    for i in range(2):
                        kc = 2 * kp + i
                        pl.inc("tensor", lambda e, pO=pO, h=h, kc=kc, pg=pg, i=i: e.matmul(
                            pO, v4[:, h, kc, :],
                            ptil[:, pg * 1024 + i * 512: pg * 1024 + (i + 1) * 512],
                            start=(kc == 0), stop=(kc == nkc - 1)), "vav" + fx)
                pl.wait("vector", "vav" + fx)
                pl.inc("vector", lambda e, pO=pO: e.reciprocal(
                    rl_sb[64:65, :], pO[64:65, :]), "vr" + fx)
                pl.wait("tensor", "vr" + fx)
                pLb = self.bank(6)[0:64, :]
                pl.inc("tensor", lambda e, pLb=pLb: e.matmul(
                    pLb, self.c_ones[64:65, 0:64], rl_sb[64:65, :],
                    start=True, stop=True), sT)
                pl.wait("scalar", sT)
                pl.inc("scalar", lambda e, pLb=pLb: e.copy(lb_sb[0:64, :], pLb), "sl" + fx)
                pl.wait("vector", "sl" + fx)
                pl.inc("vector", lambda e, pO=pO, h=h, oc3=oc3: e.tensor_tensor(
                    oc3[(h % 2) * 64:(h % 2) * 64 + 64, h // 2, :],
                    pO[0:64, :], lb_sb[0:64, :], AO.mult), "so" + fx)
            if self.debug and b == 0 and qc == 0:
                pl.wait("gpsimd", "so" + fx)
                self.tap("p3_oall_qc0", o_all)
            pl.wait("tensor", "so" + fx)
            pl.wait("tensor", "dG")
            wo3 = wo_sb.rearrange("p (c n) -> p c n", c=8)
            for tp4 in range(4):
                tpg = qc * 4 + tp4
                par = tp4 % 2
                pb = self.bank(5 + par, 1)
                pb = self.bank(6 if par else 2, 2)
                h2b = h2sb[:, par * 1024:(par + 1) * 1024]
                tb = tsb[:, par * 1024:(par + 1) * 1024]
                gb = gtm[:, par * 1024:(par + 1) * 1024]
                if tpg >= 2:
                    pl.wait("gpsimd", sV)
                self.dma("gpsimd", h2b, self.h2_s[b, tpg * 128:(tpg + 1) * 128, :], "dH" + fx)
                if tp4 >= 2:
                    pl.wait("tensor", sV)
                for dc in range(8):
                    for pc in range(2):
                        pl.inc("tensor", lambda e, dc=dc, pc=pc, pb=pb, tp4=tp4, oc3=oc3: e.matmul(
                            pb[:, pc * 512:(pc + 1) * 512],
                            oc3[:, dc, tp4 * 128:(tp4 + 1) * 128],
                            wo3[:, dc, pc * 512:(pc + 1) * 512],
                            start=(dc == 0), stop=(dc == 7)), sT)
                pl.wait("vector", sT)
                pl.wait("vector", "dH" + fx)
                if tpg >= 2:
                    pl.wait("vector", "dT" + fx)
                pl.inc("vector", lambda e, tb=tb, pb=pb, h2b=h2b: e.tensor_tensor(
                    tb, pb, h2b, AO.add), sV)
                pl.wait("gpsimd", sV)
                self.dma("gpsimd", self.t_s[b, tpg * 128:(tpg + 1) * 128, :], tb, "dT" + fx)
                pl.wait("scalar", sV)
                ssq = p3st[:, 0:1]; rr = p3st[:, 1:2]
                pl.inc("scalar", lambda e, tb=tb, ssq=ssq: e.activation(
                    p3sq, tb, AF.Square, accum_out=ssq), "s3" + fx)
                pl.wait("scalar", "s3" + fx)
                pl.inc_wait("scalar", lambda e, ssq=ssq: e.activation(
                    ssq, ssq, AF.Ln, scale=1.0 / D, bias=self.c_eps))
                pl.inc_wait("scalar", lambda e, ssq=ssq, rr=rr: e.activation(
                    rr, ssq, AF.Exp, scale=-0.5))
                if tpg >= 2:
                    pl.wait("scalar", "dGs" + fx)
                pl.inc("scalar", lambda e, gb=gb, tb=tb, rr=rr: e.activation(
                    gb, tb, AF.Identity, scale=rr, bias=0.0), "s3" + fx)
                pl.wait("gpsimd", "s3" + fx)
                self.dma("gpsimd", self.g_s[b, tpg * 128:(tpg + 1) * 128, :], gb, "dGs" + fx)
                if self.debug and b == 0 and tpg < 2:
                    pl.wait("gpsimd", sV)
                    self.tap(f"p3_t_tp{tpg}", tb)
                    self.tap(f"p3_g_tp{tpg}", gb)

    def emit_P4(self, b):
        if STAGE < 100:
            return
        pl = self.pl
        s = self.sbv
        O = self.OFF_P
        fx = "4"
        sV, sS, sT = "v" + fx, "s" + fx, "t" + fx
        fc1sb = s(O, 8 * 2048, BF16)
        fc2sb = s(O + 32 * 1024, 16 * 1024, BF16)
        h1act = s(O + 64 * 1024, 16 * 1024, BF16)
        g_fm = s(O + 96 * 1024, 8 * 2048, BF16)
        acc = s(O + 128 * 1024, 8 * 1024, F32)
        crep = s(O + 160 * 1024, 1024, BF16)
        c_fm = s(O + 162 * 1024, 2048, BF16, rows=8)
        gld = s(O + 166 * 1024, 2 * 1024, BF16)
        tld = s(O + 170 * 1024, 2 * 1024, F32)
        x2ld = s(O + 178 * 1024, 2 * 1024, F32)
        p4sq = s(O + 186 * 1024, 1024, F32)
        p4st = s(O + 190 * 1024, 16, F32)
        cstg = s(O + 191 * 1024, 1024, BF16, rows=1)
        pl.wait("sync", "dWc")
        pl.wait("gpsimd", "dWc")
        pl.wait("gpsimd", "t3")
        pl.wait("gpsimd", "s3")
        pl.wait("gpsimd", "v3")
        pl.wait("sync", "t3")
        pl.wait("sync", "s3")
        pl.wait("sync", "dT3")
        pl.wait("sync", "dX22")
        sR = "sr" + fx
        for tp in range(16):
            par = tp % 2
            gb = gld[:, par * 1024:(par + 1) * 1024]
            if tp >= 2:
                pl.wait("gpsimd", sT)
            self.dma("gpsimd", gb, self.g_s[b, tp * 128:(tp + 1) * 128, :], "dGl" + fx)
            pl.wait("tensor", "dGl" + fx)
            for c in range(8):
                pbt = self.bank(4)[:, 64 * (c % 2):64 * (c % 2) + 64].bitcast(BF16)
                if c >= 2 or tp > 0:
                    pl.wait("tensor", sS)
                pl.inc("tensor", lambda e, c=c, pbt=pbt, gb=gb: e.transpose(
                    pbt, gb[:, c * 128:(c + 1) * 128], self.c_ident), sT)
                pl.wait("scalar", sT)
                pl.inc("scalar", lambda e, c=c, pbt=pbt, tp=tp: e.copy(
                    g_fm[:, c * 2048 + tp * 128: c * 2048 + (tp + 1) * 128], pbt), sS)
        g3 = g_fm.rearrange("p (c n) -> p c n", c=8)
        r3 = self.c_router.rearrange("p (c n) -> p c n", c=8)
        rtm = p4st
        pl.wait("tensor", sS)
        for tp in range(16):
            pR = self.bank(7)[:, 0:8]
            if tp > 0:
                pl.wait("tensor", "vR" + fx)
                pl.wait("tensor", sS)
            for f in range(8):
                pl.inc("tensor", lambda e, f=f, tp=tp, pR=pR: e.matmul(
                    pR, g3[:, f, tp * 128:(tp + 1) * 128], r3[:, f, :],
                    start=(f == 0), stop=(f == 7)), sT)
            pl.wait("vector", sT)
            lg = rtm[:, 0:8]
            m1 = rtm[:, 8:9]; m2 = rtm[:, 9:10]; dd = rtm[:, 10:11]
            den = rtm[:, 11:12]
            sc8 = p4sq[:, 0:8]
            e8 = p4sq[:, 8:16]
            cb8 = p4sq[:, 16:24].bitcast(BF16)[:, 0:8]
            pl.inc_wait("vector", lambda e, pR=pR, lg=lg: e.tensor_copy(lg, pR))
            pl.inc_wait("vector", lambda e, lg=lg, m1=m1: e.reduce_max(m1, lg, AX.X))
            pl.inc("vector", lambda e, sc8=sc8, lg=lg, m1=m1: e.tensor_scalar(
                sc8, lg, m1, None, AO.subtract), "vR" + fx)
            pl.wait("scalar", "vR" + fx)
            pl.inc("scalar", lambda e, sc8=sc8: e.activation(sc8, sc8, AF.Sign), "sR" + fx)
            pl.wait("vector", "sR" + fx)
            pl.inc_wait("vector", lambda e, sc8=sc8: e.tensor_scalar(sc8, sc8, 1.0, None, AO.add))
            pl.inc_wait("vector", lambda e, sc8=sc8, lg=lg: e.scalar_tensor_tensor(
                sc8, sc8, -1e30, lg, AO.mult, AO.add))
            pl.inc_wait("vector", lambda e, sc8=sc8, m2=m2: e.reduce_max(m2, sc8, AX.X))
            pl.inc("vector", lambda e, dd=dd, m2=m2, m1=m1: e.tensor_tensor(
                dd, m2, m1, AO.subtract), "vR" + fx)
            pl.wait("scalar", "vR" + fx)
            pl.inc("scalar", lambda e, dd=dd, den=den: e.activation(den, dd, AF.Exp), "sR" + fx)
            pl.wait("vector", "sR" + fx)
            pl.inc_wait("vector", lambda e, den=den: e.tensor_scalar(den, den, 1.0, None, AO.add))
            pl.inc_wait("vector", lambda e, den=den: e.reciprocal(den, den))
            pl.inc("vector", lambda e, sc8=sc8, lg=lg, m2=m2: e.tensor_scalar(
                sc8, lg, m2, None, AO.subtract), "vR" + fx)
            pl.wait("scalar", "vR" + fx)
            pl.inc("scalar", lambda e, sc8=sc8: e.activation(sc8, sc8, AF.Sign), "sR" + fx)
            pl.wait("vector", "sR" + fx)
            pl.inc_wait("vector", lambda e, sc8=sc8: e.tensor_scalar(
                sc8, sc8, 1.0, 1.0, AO.add, op1=AO.min))
            pl.inc("vector", lambda e, lg=lg, m1=m1, e8=e8: e.tensor_scalar(
                e8, lg, m1, None, AO.subtract), "vR" + fx)
            pl.wait("scalar", "vR" + fx)
            pl.inc("scalar", lambda e, e8=e8: e.activation(e8, e8, AF.Exp), "sR" + fx)
            pl.wait("vector", "sR" + fx)
            pl.inc_wait("vector", lambda e, e8=e8, sc8=sc8: e.tensor_tensor(e8, e8, sc8, AO.mult))
            pl.inc("vector", lambda e, e8=e8, den=den, cb8=cb8: e.tensor_scalar(
                cb8, e8, den, None, AO.mult), "vR" + fx)
            pl.wait("tensor", "vR" + fx)
            pc8 = self.bank(6)[0:8, 0:64].bitcast(BF16)[:, 0:128]
            pl.inc("tensor", lambda e, cb8=cb8, pc8=pc8: e.transpose(
                pc8, cb8, self.c_ident), sT)
            pl.wait("scalar", sT)
            pl.inc("scalar", lambda e, pc8=pc8, tp=tp: e.copy(
                c_fm[0:8, tp * 128:(tp + 1) * 128], pc8), "sR" + fx)
        f13 = fc1sb.rearrange("p (f n) -> p f n", f=8)
        f23 = fc2sb.rearrange("p (c n) -> p c n", c=16)
        h13 = h1act.rearrange("p (c n) -> p c n", c=16)
        a3 = acc.rearrange("p (c n) -> p c n", c=8)
        if self.debug and b == 0:
            pl.wait("gpsimd", "sR" + fx)
            self.tap("p4_gfm", g_fm)
            self.tap("p4_cfm", c_fm)
        for grp in range(2):
            gcol = grp * 1024
            pl.wait("vector", sT)
            pl.inc("vector", lambda e: e.memset(acc, 0.0), sV)
            for e_i in range(NE):
                pl.wait("gpsimd", sT)
                self.dma("gpsimd", f13, self.wbf1[e_i].rearrange("c p n -> p c n"), "dW" + fx)
                self.dma("gpsimd", f23, self.wbf2[e_i].rearrange("c p n -> p c n"), "dW" + fx)
                
                pl.wait("tensor", "s3")
                if e_i > 0 or grp > 0:
                    pl.wait("gpsimd", "vh" + fx)
                    pl.wait("gpsimd", "tc" + fx)
                self.dma("gpsimd", cstg, c_fm[e_i:e_i + 1, gcol:gcol + 1024], "dC" + fx)
                pl.wait("tensor", "dC" + fx)
                pcb = self.bank(6, 2)
                if e_i > 0 or grp > 0:
                    pl.wait("tensor", "sc" + fx)
                for pc in range(2):
                    pl.inc("tensor", lambda e, pc=pc, pcb=pcb: e.matmul(
                        pcb[:, pc * 512:(pc + 1) * 512], self.c_onesb[0:1, :],
                        cstg[0:1, pc * 512:(pc + 1) * 512], start=True, stop=True), "tc" + fx)
                pl.wait("scalar", "tc" + fx)
                pl.inc("scalar", lambda e, pcb=pcb: e.copy(crep, pcb), "sc" + fx)
                pl.wait("tensor", "dW" + fx)
                for hp in range(16):
                    pg = hp % 2
                    pH = self.bank(2 * pg, 2)
                    if hp >= 2:
                        pl.wait("tensor", "sh" + fx)
                    for f in range(8):
                        for tc2 in range(2):
                            pl.inc("tensor", lambda e, f=f, hp=hp, tc2=tc2, pH=pH, gcol=gcol: e.matmul(
                                pH[:, tc2 * 512:(tc2 + 1) * 512],
                                f13[:, f, hp * 128:(hp + 1) * 128],
                                g3[:, f, gcol + tc2 * 512: gcol + (tc2 + 1) * 512],
                                start=(f == 0), stop=(f == 7)), sT)
                    pl.wait("scalar", sT)
                    pl.inc("scalar", lambda e, hp=hp, pH=pH: e.activation(
                        h13[:, hp, :], pH, AF.Relu), "sh" + fx)
                    pl.wait("vector", "sh" + fx)
                    pl.inc("vector", lambda e, hp=hp: e.tensor_tensor(
                        h13[:, hp, :], h13[:, hp, :], h13[:, hp, :], AO.mult), "vh" + fx)
                    pl.wait("vector", "sc" + fx)
                    pl.inc("vector", lambda e, hp=hp: e.tensor_tensor(
                        h13[:, hp, :], h13[:, hp, :], crep, AO.mult), "vh" + fx)
                pl.wait("tensor", "vh" + fx)
                for dp in range(8):
                    pg = dp % 2
                    pF = self.bank(2 * pg, 2)
                    if dp >= 2 or True:
                        pl.wait("tensor", sV)
                    for hc in range(16):
                        for tc2 in range(2):
                            pl.inc("tensor", lambda e, dp=dp, hc=hc, tc2=tc2, pF=pF: e.matmul(
                                pF[:, tc2 * 512:(tc2 + 1) * 512],
                                f23[:, hc, dp * 128:(dp + 1) * 128],
                                h13[:, hc, tc2 * 512:(tc2 + 1) * 512],
                                start=(hc == 0), stop=(hc == 15)), sT)
                    pl.wait("vector", sT)
                    pl.inc("vector", lambda e, dp=dp, pF=pF: e.tensor_tensor(
                        a3[:, dp, :], a3[:, dp, :], pF, AO.add), sV)
            if self.debug and b == 0 and grp == 0:
                pl.wait("gpsimd", sV)
                pl.wait("gpsimd", "vh" + fx)
                self.tap("p4_h1act_e7", h1act)
                self.tap("p4_acc_g0", acc)
                self.tap("p4_fc1sb_e7", fc1sb)
                self.tap("p4_fc2sb_e7", fc2sb)
                self.tap("p4_crep_e7", crep)
            for tp4 in range(8):
                tpg = grp * 8 + tp4
                par = tp4 % 2
                tb = tld[:, par * 1024:(par + 1) * 1024]
                x2b = x2ld[:, par * 1024:(par + 1) * 1024]
                if tp4 >= 2 or grp > 0:
                    pl.wait("sync", sV)
                    pl.wait("sync", "dO")
                self.dma("sync", tb, self.t_s[b, tpg * 128:(tpg + 1) * 128, :], "dF" + fx)
                self.dma("sync", x2b, self.x2_s[b, tpg * 128:(tpg + 1) * 128, :], "dF" + fx)
                pl.wait("tensor", sV)
                pT = self.bank(2, 2)
                for dc in range(8):
                    pl.inc("tensor", lambda e, dc=dc, tp4=tp4, pT=pT: e.transpose(
                        pT[:, dc * 128:(dc + 1) * 128],
                        a3[:, dc, tp4 * 128:(tp4 + 1) * 128], self.c_identf), sT)
                pl.wait("vector", sT)
                pl.wait("vector", "dF" + fx)
                pl.inc("vector", lambda e, tb=tb, pT=pT: e.tensor_tensor(
                    tb, tb, pT, AO.add), sV)
                pl.wait("vector", sV)
                mu = p4st[:, 0:1]; ssq = p4st[:, 1:2]; rr = p4st[:, 2:3]; mb2 = p4st[:, 3:4]
                pl.inc_wait("vector", lambda e, mu=mu, tb=tb: e.reduce_sum(mu, tb, AX.X))
                pl.wait("scalar", sV)
                pl.inc("scalar", lambda e, ssq=ssq, tb=tb: e.activation(
                    p4sq, tb, AF.Square, accum_out=ssq), sS)
                pl.inc_wait("vector", lambda e, mu=mu: e.tensor_scalar(mu, mu, 1.0 / D, None, AO.mult))
                pl.wait("vector", sS)
                pl.inc_wait("vector", lambda e, ssq=ssq: e.tensor_scalar(ssq, ssq, 1.0 / D, EPS, AO.mult, op1=AO.add))
                pl.inc_wait("vector", lambda e, rr=rr, mu=mu: e.tensor_tensor(rr, mu, mu, AO.mult))
                pl.inc("vector", lambda e, ssq=ssq, rr=rr: e.tensor_tensor(ssq, ssq, rr, AO.subtract), sV)
                pl.wait("scalar", sV)
                pl.inc_wait("scalar", lambda e, ssq=ssq: e.activation(ssq, ssq, AF.Ln))
                pl.inc("scalar", lambda e, ssq=ssq, rr=rr: e.activation(rr, ssq, AF.Exp, scale=-0.5), sS)
                pl.wait("vector", sS)
                pl.inc_wait("vector", lambda e, mb2=mb2, mu=mu, rr=rr: e.tensor_tensor(mb2, mu, rr, AO.mult))
                pl.inc("vector", lambda e, mb2=mb2: e.tensor_scalar(mb2, mb2, -1.0, None, AO.mult), sV)
                pl.wait("scalar", sV)
                pl.inc("scalar", lambda e, rr=rr, mb2=mb2, tb=tb: e.activation(
                    p4sq, tb, AF.Identity, scale=rr, bias=mb2), sS)
                pl.wait("vector", sS)
                if tp4 >= 2:
                    pl.wait("vector", "dO")
                pl.inc("vector", lambda e, x2b=x2b: e.tensor_tensor(
                    x2b, p4sq, x2b, AO.add), sV)
                pl.wait("gpsimd", sV)
                self.dma("gpsimd", self.y_out[b, tpg * 128:(tpg + 1) * 128, :], x2b, "dO")

    def emit_final(self):
        pl = self.pl
        if STAGE < 100:
            z = self.sbv(self.OFF_P, 1024, F32)
            pl.wait("sync", "dS")
            for b in range(BPC):
                for tt in range(L // 128):
                    self.dma("sync", z, self.x_in[b, tt * 128:(tt + 1) * 128, :], "dF")
                    pl.wait("sync", "dF")
                    self.dma("sync", self.y_out[b, tt * 128:(tt + 1) * 128, :], z, "dO")
        pl.wait("gpsimd", "dO")
        pl.wait("sync", "dWc")
        pl.wait("gpsimd", "dS")
        if "dDBG" in pl.cnt:
            pl.wait("gpsimd", "dDBG")


_cache = {}


def build_program(debug=False):
    key = "dbg" if debug else "plain"
    if key not in _cache:
        b = Builder(debug=debug)
        _cache[key] = (b.build(), b.dbg_outs)
    return _cache[key]


WMAP = {
    "W_in": "L0_attn_W_in", "conv_w": "L0_attn_conv_w", "conv_b": "L0_attn_conv_b",
    "W_xproj": "L0_attn_W_xproj", "W_dt": "L0_attn_W_dt", "b_dt": "L0_attn_b_dt",
    "A_log": "L0_attn_A_log", "Dp": "L0_attn_Dp", "W_out": "L0_attn_W_out",
    "wq": "L0_ffn_wq", "wk": "L0_ffn_wk", "wv": "L0_ffn_wv", "wo": "L0_ffn_wo",
    "router_w": "L0_ffn_router_w", "exp_fc1": "L0_ffn_exp_fc1",
    "exp_fc2": "L0_ffn_exp_fc2",
}


def run(inputs, debug=False):
    x = np.ascontiguousarray(inputs["L0_attn_x"])
    nc, dbg = build_program(debug=debug)
    in_maps = []
    for c in range(8):
        m = {"x": x[c * BPC:(c + 1) * BPC]}
        for k, v in WMAP.items():
            m[k] = np.ascontiguousarray(inputs[v])
        in_maps.append(m)
    res = run_bass_kernel_spmd(nc, in_maps, core_ids=list(range(8)))
    return res


def kernel(**inputs):
    res = run(inputs)
    return np.concatenate([np.asarray(r["y"]) for r in res.results], axis=0)


# version 06 (final)
# speedup vs baseline: 1.0168x; 1.0168x over previous
import contextlib
import os
import numpy as np

import concourse.bass as bass
import concourse.mybir as mybir
from concourse.bass_utils import run_bass_kernel_spmd

F32 = mybir.dt.float32
BF16 = mybir.dt.bfloat16
U8 = mybir.dt.uint8
AO = mybir.AluOpType
AF = mybir.ActivationFunctionType
AX = mybir.AxisListType

STAGE = int(os.environ.get("K_STAGE", "100"))
L, D, DI, NS = 2048, 1024, 2048, 16
NH, DH, NE, ED = 16, 64, 8, 2048
BPC = 2
EPS = 1e-5
TC1 = 256
NC1 = L // TC1


class Plan:

    def __init__(self, nc):
        self.nc = nc
        self.ops = {"sync": [], "tensor": [], "vector": [], "scalar": [], "gpsimd": []}
        self.cnt = {}
        self.sems = {}

    def sem(self, name):
        if name not in self.sems:
            self.sems[name] = self.nc._stack.enter_context(self.nc.semaphore(name))
            self.cnt[name] = 0
        return self.sems[name]

    def on(self, eng, fn):
        self.ops[eng].append(fn)

    def inc(self, eng, fn, sem_name, amt=1):
        s = self.sem(sem_name)
        self.cnt[sem_name] += amt

        def go(e, _fn=fn, _s=s, _amt=amt):
            _fn(e).then_inc(_s, _amt)
        self.ops[eng].append(go)
        return self.cnt[sem_name]

    def inc_wait(self, eng, fn, tag="iw"):
        name = tag + "_" + eng
        v = self.inc(eng, fn, name)
        self.wait(eng, name, v)

    def selfbar(self, eng, tag="bar"):
        name = tag + "_" + eng
        v = self.inc(eng, lambda e: e.nop(), name)
        self.wait(eng, name, v)

    def wait(self, eng, sem_name, value=None):
        s = self.sem(sem_name)
        v = self.cnt[sem_name] if value is None else value
        if v <= 0:
            return

        def go(e, _s=s, _v=v):
            e.wait_ge(_s, _v)
        self.ops[eng].append(go)

    def emit(self):
        blk = self.nc._stack.enter_context(self.nc.Block())
        for eng_name in ["sync", "tensor", "vector", "scalar", "gpsimd"]:
            ops = self.ops[eng_name]
            dec = getattr(blk, eng_name)

            def make(ops_list):
                def body(eng):
                    for fn in ops_list:
                        fn(eng)
                return body
            dec(make(ops))


def cdiv(a, b):
    return (a + b - 1) // b


def align(o, a=512):
    return cdiv(o, a) * a


class Builder:
    def __init__(self, debug=False):
        self.debug = debug
        nc = bass.Bass("TRN2", target_bir_lowering=False)
        nc._stack = contextlib.ExitStack()
        self.nc = nc
        self.dbg_outs = {}

    def alloc(self):
        nc = self.nc
        st = nc._stack
        self.x_in = nc.dram_tensor("x", [BPC, L, D], F32, kind="ExternalInput").ap()
        shapes = {
            "W_in": [D, 2 * DI], "conv_w": [DI, 1, 4], "conv_b": [DI],
            "W_xproj": [DI, 2 * NS + 1], "W_dt": [1, DI], "b_dt": [DI],
            "A_log": [DI, NS], "Dp": [DI], "W_out": [DI, D],
            "wq": [D, D], "wk": [D, D], "wv": [D, D], "wo": [D, D],
            "router_w": [D, NE], "exp_fc1": [NE, D, ED], "exp_fc2": [NE, ED, D],
        }
        self.w = {n: nc.dram_tensor(n, s, F32, kind="ExternalInput").ap()
                  for n, s in shapes.items()}
        self.y_out = nc.dram_tensor("y", [BPC, L, D], F32, kind="ExternalOutput").ap()

        self.h1_s = nc.dram_tensor("h1_s", [BPC, 8, 128, L], BF16, kind="Internal").ap()
        self.ypre_s = nc.dram_tensor("ypre_s", [BPC, 16, 128, L], BF16, kind="Internal").ap()
        self.gpre_s = nc.dram_tensor("gpre_s", [BPC, 16, 128, L], BF16, kind="Internal").ap()
        self.x2_s = nc.dram_tensor("x2_s", [BPC, L, D], F32, kind="Internal").ap()
        self.h2_s = nc.dram_tensor("h2_s", [BPC, L, D], BF16, kind="Internal").ap()
        self.t_s = nc.dram_tensor("t_s", [BPC, L, D], F32, kind="Internal").ap()
        self.qhat_s = nc.dram_tensor("qhat_s", [BPC, NH, 65, L], BF16, kind="Internal").ap()
        self.g_s = nc.dram_tensor("g_s", [BPC, L, D], BF16, kind="Internal").ap()

        self.arena = st.enter_context(nc.sbuf_tensor("arena", [128, 206 * 1024], U8))
        self.psum = st.enter_context(nc.psum_tensor("ps", [128, 4096], F32))
        self.pl = Plan(nc)

    def sbv(self, off, cols, dtype, rows=128):
        esz = {F32: 4, BF16: 2, U8: 1, mybir.dt.int32: 4}[dtype]
        assert off % 512 == 0 and off + cols * esz <= 206 * 1024, (off, cols)
        return self.arena.ap()[0:rows, off:off + cols * esz].bitcast(dtype)

    def bank(self, i, n=1):
        return self.psum[:, i * 512:(i + n) * 512]

    def dma(self, queue, out, in_, sem, slow=False):
        if slow:
            return self.pl.inc(queue, lambda e: e.dma_start(
                out=out, in_=in_, allow_slow_non_contiguous=True), sem, 16)
        return self.pl.inc(queue, lambda e: e.dma_start(out=out, in_=in_), sem, 16)

    def tap(self, name, ap):
        rows = int(ap.shape[0])
        cols = int(np.prod([int(d) for d in ap.shape[1:]]))
        flat = ap if len(ap.shape) == 2 else None
        assert flat is not None, "tap wants 2D"
        t = self.nc.dram_tensor("dbg_" + name, [rows, cols], ap.dtype,
                                kind="ExternalOutput").ap()
        self.dbg_outs[name] = (rows, cols)
        self.dma("gpsimd", t, flat, "dDBG")
        self.pl.wait("vector", "dDBG")
        self.pl.wait("scalar", "dDBG")
        self.pl.wait("tensor", "dDBG")
        self.pl.wait("sync", "dDBG")

    def build(self):
        self.alloc()
        self.map_constants()
        self.emit_P0()
        if STAGE >= 1:
            for b in range(BPC):
                self.emit_P1(b)
                self.emit_P2(b)
                self.emit_P3(b)
                self.emit_P4(b)
        self.emit_final()
        self.pl.emit()
        self.nc._stack.close()
        return self.nc

    def map_constants(self):
        s = self.sbv
        o = 0
        self.c_A = s(o, 16 * 16, F32); o = align(o + 16 * 16 * 4)
        self.c_Wdt = s(o, 16, F32); o = align(o + 64)
        self.c_bdt = s(o, 16, F32); o = align(o + 64)
        self.c_Dp = s(o, 16, F32); o = align(o + 64)
        self.c_convb = s(o, 16, F32); o = align(o + 64)
        self.c_convw = s(o, 16 * 4, F32); o = align(o + 256)
        self.c_xproj = s(o, 16 * 33, BF16); o = align(o + 16 * 33 * 2)
        self.c_ones = s(o, 128, F32); o = align(o + 512)
        self.c_onesb = s(o, 128, BF16); o = align(o + 256)
        self.c_ident = s(o, 128, BF16); o = align(o + 256)
        self.c_identf = s(o, 128, F32); o = align(o + 512)
        self.c_mask01 = s(o, 4 * 512, BF16); o = align(o + 4 * 512 * 2)
        self.c_router = s(o, 8 * 8, BF16); o = align(o + 128)
        self.c_onescol = s(o, 1, BF16); o = align(o + 2)
        self.c_eps = s(o, 1, F32); o = align(o + 4)
        self.c_ones2db = s(o, 128, BF16); o = align(o + 256)
        self.c_onesf64 = None
        self.OFF_P = align(o)
        assert self.OFF_P <= 16 * 1024, self.OFF_P
        self.OFF_STG = 190 * 1024

    def emit_P0(self):
        pl, w = self.pl, self.w
        self.dma("sync", self.c_A.rearrange("p (c s) -> p c s", c=16),
                 w["A_log"].rearrange("(c p) s -> p c s", p=128), "dW")
        self.dma("sync", self.c_Wdt, w["W_dt"].rearrange("o (c p) -> p (o c)", p=128), "dW", slow=True)
        self.dma("sync", self.c_bdt, w["b_dt"].rearrange("(c p) -> p c", p=128), "dW", slow=True)
        self.dma("sync", self.c_Dp, w["Dp"].rearrange("(c p) -> p c", p=128), "dW", slow=True)
        self.dma("sync", self.c_convb, w["conv_b"].rearrange("(c p) -> p c", p=128), "dW", slow=True)
        self.dma("sync", self.c_convw.rearrange("p (c k) -> p c k", c=16),
                 w["conv_w"].rearrange("(c p) o k -> p c (o k)", p=128), "dW")
        self.dma("gpsimd", self.c_xproj.rearrange("p (c n) -> p c n", c=16),
                 w["W_xproj"].rearrange("(c p) n -> p c n", p=128), "dG")
        self.dma("gpsimd", self.c_router.rearrange("p (c n) -> p c n", c=8),
                 w["router_w"].rearrange("(c p) n -> p c n", p=128), "dG")
        pl.wait("scalar", "dW")
        pl.inc("scalar", lambda e: e.activation(self.c_A, self.c_A, AF.Exp), "sC")
        pl.wait("vector", "sC")
        pl.inc("vector", lambda e: e.tensor_scalar(self.c_A, self.c_A, -1.0, None, AO.mult), "vC")

        def build_consts(g):
            g.memset(self.c_ones, 1.0)
            g.memset(self.c_onesb[0:1, :], 1.0)
            g.memset(self.c_onescol, 1.0)
            g.memset(self.c_eps, EPS)
            g.memset(self.c_ones2db, 1.0)
            g.memset(self.c_ident, 1.0)
            g.affine_select(self.c_ident, self.c_ident, pattern=[[1, 128]],
                            compare_op=AO.is_equal, fill=0.0, base=0, channel_multiplier=-1)
            g.memset(self.c_identf, 1.0)
            g.affine_select(self.c_identf, self.c_identf, pattern=[[1, 128]],
                            compare_op=AO.is_equal, fill=0.0, base=0, channel_multiplier=-1)
            m = self.c_mask01.rearrange("p (r n) -> p r n", r=4)
            g.memset(m, 1.0)
            for r in range(4):
                g.affine_select(m[:, r, :], m[:, r, :], pattern=[[1, 512]],
                                compare_op=AO.is_ge, fill=0.0, base=-128 * r,
                                channel_multiplier=-1)
            return g.nop()
        pl.inc("gpsimd", build_consts, "gC")

        pl.inc("sync", lambda e: e.nop(), "dWc")

    def emit_P1(self, b):
        pl = self.pl
        s = self.sbv
        o = self.OFF_P
        w_in = s(o, 8 * 2048, BF16); o = align(o + 8 * 2048 * 2)
        xt = s(o, 2 * 1024, F32); o = align(o + 2 * 1024 * 4)
        sq = s(o, 1024, F32); o = align(o + 4096)
        h1tm = s(o, 2 * 1024, BF16); o = align(o + 2 * 1024 * 2)
        h1fm = s(o, 8 * TC1, BF16); o = align(o + 8 * TC1 * 2)
        xcpre = s(o, 16 * (TC1 + 4), BF16); o = align(o + 16 * (TC1 + 4) * 2)
        halo = s(o, 16 * 4, BF16); o = align(o + 128)
        xcs = s(o, 16 * TC1, BF16); o = align(o + 16 * TC1 * 2)
        xdbl = s(o, TC1, F32, rows=33); o = align(o + TC1 * 4)
        bcflat = s(o, 32 * TC1, BF16, rows=1); o = align(o + 32 * TC1 * 2)
        brep = s(o, 16 * TC1, BF16); o = align(o + 16 * TC1 * 2)
        crep = s(o, 16 * TC1, BF16); o = align(o + 16 * TC1 * 2)
        delta = s(o, 16 * TC1, F32); o = align(o + 16 * TC1 * 4)
        dx = s(o, TC1, BF16); o = align(o + TC1 * 2)
        dA = s(o, NS * TC1, F32); o = align(o + NS * TC1 * 4)
        dxB = s(o, NS * TC1, BF16); o = align(o + NS * TC1 * 2)
        hbuf = s(o, NS * TC1, BF16); o = align(o + NS * TC1 * 2)
        hC = s(o, NS * TC1, BF16); o = align(o + NS * TC1 * 2)
        tre = s(o, 8 * TC1, BF16); o = align(o + 8 * TC1 * 2)
        hstate = s(o, 16 * 16, F32); o = align(o + 1024)
        ypre = s(o, 16 * TC1, BF16); o = align(o + 16 * TC1 * 2)
        stats = s(o, 16, F32); o = align(o + 64)
        assert o <= self.OFF_STG, o
        self.P1_END = o

        fx = "1"
        sV, sS, sT = "v" + fx, "s" + fx, "t" + fx
        self.dma("gpsimd", w_in.rearrange("p (f n) -> p f n", f=8),
                 self.w["W_in"].rearrange("(f p) n -> p f n", p=128)[:, :, 0:DI], "dG")
        pl.wait("tensor", "dG")
        pl.wait("tensor", "gC")
        pl.wait("vector", "vC")
        pl.wait("scalar", "dW")
        pl.inc("vector", lambda e: e.memset(hstate, 0.0), sV)
        pl.inc("vector", lambda e: e.memset(halo, 0.0), sV)
        for c in range(15):
            pl.on("vector", lambda e, c=c: e.memset(
                xcpre[:, c * (TC1 + 4):c * (TC1 + 4) + 4], 0.0))
        pl.inc("vector", lambda e: e.memset(
            xcpre[:, 15 * (TC1 + 4):15 * (TC1 + 4) + 4], 0.0), sV)

        A3 = self.c_A.rearrange("p (c s) -> p c s", c=16)
        cw = self.c_convw.rearrange("p (c k) -> p c k", c=16)
        silu_cnt = []

        for tc in range(NC1):
            t0 = tc * TC1
            mu = stats[:, 0:2]; ssq = stats[:, 2:4]; rr = stats[:, 4:6]; mb = stats[:, 6:8]
            if tc > 0:
                pl.wait("sync", sS)
            for j in range(2):
                xtile = xt[:, j * 1024:(j + 1) * 1024]
                self.dma("sync", xtile, self.x_in[b, t0 + j * 128:t0 + (j + 1) * 128, :], "dX")
            pl.wait("vector", "dX")
            pl.wait("scalar", "dX")
            pl.wait("scalar", sT)
            pl.on("vector", lambda e: e.reduce_sum(mu[:, 0:1], xt[:, 0:1024], AX.X))
            pl.inc_wait("vector", lambda e: e.reduce_sum(mu[:, 1:2], xt[:, 1024:2048], AX.X))
            for j in range(2):
                xtile = xt[:, j * 1024:(j + 1) * 1024]
                pl.inc("scalar", lambda e, xtile=xtile, j=j: e.activation(
                    sq, xtile, AF.Square, accum_out=ssq[:, j:j + 1]), sS)
            pl.inc_wait("vector", lambda e: e.tensor_scalar(mu, mu, 1.0 / D, None, AO.mult))
            pl.wait("vector", sS)
            pl.inc_wait("vector", lambda e: e.tensor_scalar(ssq, ssq, 1.0 / D, EPS, AO.mult, op1=AO.add))
            pl.inc_wait("vector", lambda e: e.tensor_tensor(rr, mu, mu, AO.mult))
            pl.inc("vector", lambda e: e.tensor_tensor(ssq, ssq, rr, AO.subtract), sV)
            pl.wait("scalar", sV)
            pl.inc_wait("scalar", lambda e: e.activation(ssq, ssq, AF.Ln))
            pl.inc("scalar", lambda e: e.activation(rr, ssq, AF.Exp, scale=-0.5), sS)
            pl.wait("vector", sS)
            pl.inc_wait("vector", lambda e: e.tensor_tensor(mb, mu, rr, AO.mult))
            pl.inc("vector", lambda e: e.tensor_scalar(mb, mb, -1.0, None, AO.mult), sV)
            pl.wait("scalar", sV)
            for j in range(2):
                xtile = xt[:, j * 1024:(j + 1) * 1024]
                pl.inc("scalar", lambda e, xtile=xtile, j=j: e.activation(
                    h1tm[:, j * 1024:(j + 1) * 1024], xtile, AF.Identity,
                    scale=rr[:, j:j + 1], bias=mb[:, j:j + 1]), sS)
            if self.debug and b == 0 and tc == 0 and STAGE >= 7:
                pl.wait("gpsimd", sS)
                self.tap("p1_xt", xt)
                self.tap("p1_stats", stats)
                self.tap("p1_h1tm", h1tm)
            if STAGE < 2:
                continue
            pl.wait("tensor", sS)
            pl.wait("tensor", sV)
            pl.wait("tensor", "sbc" + fx)
            pl.wait("scalar", "dS")
            for j in range(2):
                for c in range(8):
                    pb = self.bank(6 + (j * 8 + c) % 2)[:, 0:64].bitcast(BF16)
                    if j * 8 + c >= 2:
                        pl.wait("tensor", sS)
                    pl.inc("tensor", lambda e, j=j, c=c, pb=pb: e.transpose(
                        pb, h1tm[:, j * 1024 + c * 128:j * 1024 + (c + 1) * 128],
                        self.c_ident), sT)
                    pl.wait("scalar", sT)
                    pl.inc("scalar", lambda e, j=j, c=c, pb=pb: e.copy(
                        h1fm[:, c * TC1 + j * 128:c * TC1 + (j + 1) * 128], pb), sS)
            pl.wait("gpsimd", sS)
            self.dma("gpsimd", self.h1_s[b, :, :, t0:t0 + TC1].rearrange("c p t -> p c t"),
                     h1fm.rearrange("p (c t) -> p c t", c=8), "dS")
            if STAGE < 3:
                continue
            pl.wait("tensor", sS)
            for pn in range(16):
                pb = self.bank(pn % 4)[:, 0:TC1]
                if pn >= 4:
                    pl.wait("tensor", sS)
                for f in range(8):
                    pl.inc("tensor", lambda e, pn=pn, f=f, pb=pb: e.matmul(
                        pb, w_in.rearrange("p (f n) -> p f n", f=8)[:, f, pn * 128:(pn + 1) * 128],
                        h1fm[:, f * TC1:(f + 1) * TC1], start=(f == 0), stop=(f == 7)), sT)
                pl.wait("scalar", sT)
                pl.inc("scalar", lambda e, pn=pn, pb=pb: e.copy(
                    xcpre[:, pn * (TC1 + 4) + 4:(pn + 1) * (TC1 + 4)], pb), sS)
            if STAGE < 4:
                continue
            pl.wait("vector", sS)
            for c in range(16):
                base = c * (TC1 + 4)
                acc = sq[:, (c % 2) * TC1:(c % 2 + 1) * TC1]
                if len(silu_cnt) >= 2:
                    pl.wait("vector", "sil" + fx, silu_cnt[-2])
                pl.on("vector", lambda e, base=base, c=c, acc=acc: e.tensor_scalar(
                    acc, xcpre[:, base + 1:base + 1 + TC1], cw[:, c, 0:1], None, AO.mult))
                for k in range(1, 3):
                    pl.on("vector", lambda e, base=base, c=c, k=k, acc=acc: e.scalar_tensor_tensor(
                        acc, xcpre[:, base + 1 + k:base + 1 + k + TC1], cw[:, c, k:k + 1], acc,
                        AO.mult, AO.add))
                pl.inc("vector", lambda e, base=base, c=c, acc=acc: e.scalar_tensor_tensor(
                    acc, xcpre[:, base + 4:base + 4 + TC1], cw[:, c, 3:4], acc,
                    AO.mult, AO.add), sV)
                pl.wait("scalar", sV)
                n = pl.inc("scalar", lambda e, c=c, acc=acc: e.activation(
                    xcs[:, c * TC1:(c + 1) * TC1], acc, AF.Silu,
                    bias=self.c_convb[:, c:c + 1]), "sil" + fx)
                silu_cnt.append(n)
                if c < 15:
                    pl.on("vector", lambda e, c=c, base=base: e.tensor_copy(
                        halo[:, c * 4:(c + 1) * 4], xcpre[:, base + TC1:base + TC1 + 4]))
                else:
                    pl.inc("vector", lambda e, c=c, base=base: e.tensor_copy(
                        halo[:, c * 4:(c + 1) * 4], xcpre[:, base + TC1:base + TC1 + 4]), sV)
            for c in range(16):
                if c < 15:
                    pl.on("vector", lambda e, c=c: e.tensor_copy(
                        xcpre[:, c * (TC1 + 4):c * (TC1 + 4) + 4], halo[:, c * 4:(c + 1) * 4]))
                else:
                    pl.inc("vector", lambda e, c=c: e.tensor_copy(
                        xcpre[:, c * (TC1 + 4):c * (TC1 + 4) + 4], halo[:, c * 4:(c + 1) * 4]), sV)
            if STAGE < 5:
                continue
            pl.wait("tensor", "sil" + fx)
            pxd = self.bank(5)[0:33, 0:TC1]
            xp3 = self.c_xproj.rearrange("p (c n) -> p c n", c=16)
            for c in range(16):
                pl.inc("tensor", lambda e, c=c, pxd=pxd: e.matmul(
                    pxd, xp3[:, c, :], xcs[:, c * TC1:(c + 1) * TC1],
                    start=(c == 0), stop=(c == 15)), sT)
            pl.wait("scalar", sT)
            pl.wait("scalar", "dB")
            pl.inc("scalar", lambda e, pxd=pxd: e.copy(xdbl, pxd), sS)
            pl.wait("tensor", sS)
            pd0 = self.bank(6)[:, 0:TC1]
            pl.inc("tensor", lambda e, pd0=pd0: e.matmul(
                pd0, self.c_ones[0:1, :], xdbl[0:1, :], start=True, stop=True), sT)
            pl.wait("vector", sT)
            for c in range(16):
                pl.on("vector", lambda e, c=c, pd0=pd0: e.tensor_scalar(
                    delta[:, c * TC1:(c + 1) * TC1], pd0, self.c_Wdt[:, c:c + 1],
                    self.c_bdt[:, c:c + 1], AO.mult, op1=AO.add))
            pl.inc("vector", lambda e: e.tensor_scalar(delta, delta, 80.0, None, AO.min), sV)
            pl.wait("scalar", sV)
            pl.on("scalar", lambda e: e.activation(delta, delta, AF.Exp))
            pl.inc("scalar", lambda e: e.activation(delta, delta, AF.Ln, bias=1.0, scale=1.0), "sd" + fx)
            if STAGE < 6:
                continue
            pl.wait("gpsimd", sS)
            pl.wait("gpsimd", sT)
            self.dma("gpsimd", bcflat[0:1, :], xdbl[1:33, :], "dB")
            pl.wait("tensor", "dB")
            pl.wait("tensor", sV)
            nrep = NS * TC1
            for half, rep in ((0, brep), (1, crep)):
                for piece in range(nrep // 512):
                    pb = self.bank(6 + piece % 2)[:, 0:512]
                    src = bcflat[0:1, half * nrep + piece * 512:half * nrep + (piece + 1) * 512]
                    if piece >= 2 or half > 0:
                        pl.wait("tensor", "sbc" + fx)
                    pl.inc("tensor", lambda e, pb=pb, src=src: e.matmul(
                        pb, self.c_onesb[0:1, :], src, start=True, stop=True), sT)
                    pl.wait("scalar", sT)
                    pl.inc("scalar", lambda e, rep=rep, piece=piece, pb=pb: e.copy(
                        rep[:, piece * 512:(piece + 1) * 512], pb), "sbc" + fx)
            if STAGE < 7:
                continue
            pl.wait("vector", "dS")
            pl.wait("vector", "sd" + fx)
            pl.wait("vector", "sbc" + fx)
            pl.wait("vector", "sil" + fx)
            da3 = dA.rearrange("p (s t) -> p s t", s=NS)
            dxB3 = dxB.rearrange("p (s t) -> p s t", s=NS)
            br3 = brep.rearrange("p (s t) -> p s t", s=NS)
            cr3 = crep.rearrange("p (s t) -> p s t", s=NS)
            h3 = hbuf.rearrange("p (s t) -> p s t", s=NS)
            hC3 = hC.rearrange("p (s t) -> p s t", s=NS)
            t3 = tre.rearrange("p (s t) -> p s t", s=8)
            hst = hstate.rearrange("p (c s) -> p c s", c=16)
            tmp16 = sq[:, 512:512 + NS]
            for c in range(16):
                if c > 0:
                    pl.wait("vector", "sda" + fx)
                for si in range(NS - 1):
                    pl.on("vector", lambda e, c=c, si=si: e.tensor_scalar(
                        da3[:, si, :], delta[:, c * TC1:(c + 1) * TC1],
                        A3[:, c, si:si + 1], None, AO.mult))
                pl.inc("vector", lambda e, c=c: e.tensor_scalar(
                    da3[:, NS - 1, :], delta[:, c * TC1:(c + 1) * TC1],
                    A3[:, c, NS - 1:NS], None, AO.mult), "vda" + fx)
                pl.wait("scalar", "vda" + fx)
                pl.inc("scalar", lambda e: e.activation(dA, dA, AF.Exp), "sda" + fx)
                pl.on("vector", lambda e, c=c: e.tensor_tensor(
                    dx, delta[:, c * TC1:(c + 1) * TC1],
                    xcs[:, c * TC1:(c + 1) * TC1], AO.mult))
                dxv = dx.unsqueeze(1).broadcast_to((128, NS, TC1))
                pl.on("vector", lambda e, dxv=dxv: e.tensor_tensor(dxB3, dxv, br3, AO.mult))
                pl.wait("vector", "sda" + fx)
                pl.inc_wait("vector", lambda e, c=c: e.tensor_tensor(
                    tmp16, da3[:, :, 0], hst[:, c, :], AO.mult))
                pl.inc_wait("vector", lambda e: e.tensor_tensor(
                    dxB3[:, :, 0], dxB3[:, :, 0], tmp16, AO.add))
                pl.inc_wait("vector", lambda e: e.memset(da3[:, :, 0], 0.0))
                pl.inc_wait("vector", lambda e: e.tensor_tensor_scan(
                    hbuf, dA, dxB, 0.0, AO.mult, AO.add))
                pl.inc_wait("vector", lambda e, c=c: e.tensor_copy(hst[:, c, :], h3[:, :, TC1 - 1]))
                pl.on("vector", lambda e: e.tensor_tensor(hC3, h3, cr3, AO.mult))
                pl.on("vector", lambda e: e.tensor_tensor(t3, hC3[:, 0:8, :], hC3[:, 8:16, :], AO.add))
                pl.on("vector", lambda e: e.tensor_tensor(t3[:, 0:4, :], t3[:, 0:4, :], t3[:, 4:8, :], AO.add))
                pl.on("vector", lambda e: e.tensor_tensor(t3[:, 0:2, :], t3[:, 0:2, :], t3[:, 2:4, :], AO.add))
                pl.on("vector", lambda e: e.tensor_tensor(t3[:, 0:1, :], t3[:, 0:1, :], t3[:, 1:2, :], AO.add))
                pl.inc("vector", lambda e, c=c: e.scalar_tensor_tensor(
                    ypre[:, c * TC1:(c + 1) * TC1], xcs[:, c * TC1:(c + 1) * TC1],
                    self.c_Dp[:, c:c + 1], t3[:, 0, :], AO.mult, AO.add), sV)
            pl.wait("gpsimd", sV)
            self.dma("gpsimd", self.ypre_s[b, :, :, t0:t0 + TC1].rearrange("c p t -> p c t"),
                     ypre.rearrange("p (c t) -> p c t", c=16), "dS")
            if self.debug and b == 0 and tc < 2 and STAGE >= 7:
                self.tap(f"p1_ypre_tc{tc}", ypre)
                if tc == 0:
                    self.tap("p1_xcpre", xcpre)
                    self.tap("p1_winsb", w_in)
                    self.tap("p1_xproj_const", self.c_xproj)
                    self.tap("p1_xdbl", xdbl)
                    self.tap("p1_h1fm", h1fm)
                    self.tap("p1_xcs", xcs)
                    self.tap("p1_delta", delta)
                    self.tap("p1_brep", brep)
                    self.tap("p1_h", hbuf)


    def emit_P2(self, b):
        if STAGE < 80:
            return
        pl = self.pl
        s = self.sbv
        O = self.OFF_P
        fx = "2"
        sV, sS, sT = "v" + fx, "s" + fx, "t" + fx

        za_h1 = s(O, 8 * 2048, BF16)
        za_wz = s(O + 32 * 1024, 8 * 2048, BF16)
        zr_yp = s(O + 64 * 1024, 2 * 2048, BF16)
        zr_gp = s(O + 80 * 1024, 2 * 2048, BF16)
        pl.wait("gpsimd", "v1")
        pl.wait("gpsimd", "s1")
        pl.wait("gpsimd", "t1")
        pl.wait("sync", "v1")
        pl.wait("sync", "t1")
        self.dma("gpsimd", za_wz.rearrange("p (f n) -> p f n", f=8),
                 self.w["W_in"].rearrange("(f p) n -> p f n", p=128)[:, :, DI:], "dG")
        pl.wait("sync", "dS")
        self.dma("sync", za_h1.rearrange("p (f n) -> p f n", f=8),
                 self.h1_s[b].rearrange("f p t -> p f t"), "dA" + fx)
        pl.wait("tensor", "dG")
        pl.wait("tensor", "dA" + fx)
        h13 = za_h1.rearrange("p (f n) -> p f n", f=8)
        wz3 = za_wz.rearrange("p (f n) -> p f n", f=8)
        for zp in range(16):
            pb = self.bank(4 * (zp % 2), 4)
            if zp >= 2:
                pl.wait("tensor", sS)
            yb = zr_yp[:, (zp % 2) * 2048:(zp % 2 + 1) * 2048]
            if zp >= 2:
                pl.wait("sync", sV)
            self.dma("sync", yb.rearrange("p n -> p n"), self.ypre_s[b, zp], "dY" + fx)
            for f in range(8):
                for pc in range(4):
                    pl.inc("tensor", lambda e, zp=zp, f=f, pc=pc, pb=pb: e.matmul(
                        pb[:, pc * 512:(pc + 1) * 512],
                        wz3[:, f, zp * 128:(zp + 1) * 128],
                        h13[:, f, pc * 512:(pc + 1) * 512],
                        start=(f == 0), stop=(f == 7)), sT)
            pl.wait("scalar", sT)
            gb = zr_gp[:, (zp % 2) * 2048:(zp % 2 + 1) * 2048]
            if zp >= 2:
                pl.wait("scalar", "dZ" + fx)
            pl.inc("scalar", lambda e, pb=pb, gb=gb: e.activation(gb, pb, AF.Silu), sS)
            pl.wait("vector", sS)
            pl.wait("vector", "dY" + fx)
            pl.inc("vector", lambda e, gb=gb, yb=yb: e.tensor_tensor(gb, gb, yb, AO.mult), sV)
            pl.wait("gpsimd", sV)
            self.dma("gpsimd", self.gpre_s[b, zp], gb, "dZ" + fx)

        if STAGE < 85:
            return
        wb_wout = s(O, 16 * 1024, BF16)
        wb_gs = s(O + 32 * 1024, 2 * 16 * 128, BF16)
        wb_x = s(O + 40 * 1024, 2 * 1024, F32)
        wb_x2 = s(O + 48 * 1024, 2 * 1024, F32)
        wb_h2 = s(O + 56 * 1024, 2 * 1024, BF16)
        wb_rmst = s(O + 60 * 1024, 2 * 1024, BF16)
        wb_sq = s(O + 64 * 1024, 1024, F32)
        wb_st = s(O + 68 * 1024, 16, F32)
        rms_fm = s(O + 72 * 1024, 8 * 2048, BF16)
        pl.wait("gpsimd", sT)
        pl.wait("gpsimd", sV)
        pl.wait("sync", sT)
        pl.wait("sync", sS)
        self.dma("gpsimd", wb_wout.rearrange("p (c n) -> p c n", c=16),
                 self.w["W_out"].rearrange("(c p) n -> p c n", p=128), "dG")
        pl.wait("tensor", "dG")
        pl.wait("sync", "dZ" + fx)
        wo3 = wb_wout.rearrange("p (c n) -> p c n", c=16)
        st_ = wb_st
        for tp in range(16):
            par = tp % 2
            pb = self.bank(2 * par, 2)
            gsb = wb_gs[:, par * 2048:(par + 1) * 2048].rearrange("p (c n) -> p c n", c=16)
            xb = wb_x[:, par * 1024:(par + 1) * 1024]
            x2b = wb_x2[:, par * 1024:(par + 1) * 1024]
            h2b = wb_h2[:, par * 1024:(par + 1) * 1024]
            rmb = wb_rmst[:, par * 1024:(par + 1) * 1024]
            if tp >= 2:
                pl.wait("sync", sT)
                pl.wait("sync", sV)
            self.dma("sync", gsb, self.gpre_s[b, :, :, tp * 128:(tp + 1) * 128].rearrange("c p t -> p c t"), "dA" + fx)
            self.dma("sync", xb, self.x_in[b, tp * 128:(tp + 1) * 128, :], "dA" + fx)
            pl.wait("tensor", "dA" + fx)
            pl.wait("tensor", "dZ" + fx)
            if tp >= 2:
                pl.wait("tensor", sS)
            for dc in range(16):
                for pc in range(2):
                    pl.inc("tensor", lambda e, dc=dc, pc=pc, pb=pb, gsb=gsb: e.matmul(
                        pb[:, pc * 512:(pc + 1) * 512], gsb[:, dc, :],
                        wo3[:, dc, pc * 512:(pc + 1) * 512],
                        start=(dc == 0), stop=(dc == 15)), sT)
            pl.wait("vector", sT)
            pl.wait("scalar", sT)
            if STAGE < 86:
                pl.inc("scalar", lambda e, pb=pb: e.copy(wb_sq, pb), sS)
                continue
            mu = st_[:, 0:1]; ssq = st_[:, 1:2]; rr = st_[:, 2:3]; mb2 = st_[:, 3:4]
            pl.inc_wait("vector", lambda e, pb=pb, mu=mu: e.reduce_sum(mu, pb, AX.X))
            pl.wait("scalar", "iw_vector")
            pl.inc("scalar", lambda e, pb=pb, ssq=ssq: e.activation(
                wb_sq, pb, AF.Square, accum_out=ssq), sS)
            pl.inc_wait("vector", lambda e, mu=mu: e.tensor_scalar(mu, mu, 1.0 / D, None, AO.mult))
            pl.wait("vector", sS)
            pl.inc_wait("vector", lambda e, ssq=ssq: e.tensor_scalar(ssq, ssq, 1.0 / D, EPS, AO.mult, op1=AO.add))
            pl.inc_wait("vector", lambda e, rr=rr, mu=mu: e.tensor_tensor(rr, mu, mu, AO.mult))
            pl.inc("vector", lambda e, ssq=ssq, rr=rr: e.tensor_tensor(ssq, ssq, rr, AO.subtract), sV)
            pl.wait("scalar", sV)
            pl.inc_wait("scalar", lambda e, ssq=ssq: e.activation(ssq, ssq, AF.Ln))
            pl.inc("scalar", lambda e, ssq=ssq, rr=rr: e.activation(rr, ssq, AF.Exp, scale=-0.5), sS)
            pl.wait("vector", sS)
            pl.inc_wait("vector", lambda e, mb2=mb2, mu=mu, rr=rr: e.tensor_tensor(mb2, mu, rr, AO.mult))
            pl.inc("vector", lambda e, mb2=mb2: e.tensor_scalar(mb2, mb2, -1.0, None, AO.mult), sV)
            pl.wait("scalar", sV)
            pl.inc("scalar", lambda e, pb=pb, rr=rr, mb2=mb2: e.activation(
                wb_sq, pb, AF.Identity, scale=rr, bias=mb2), sS)
            pl.wait("vector", sS)
            pl.wait("vector", "dA" + fx)
            if tp >= 2:
                pl.wait("vector", "dX2" + fx)
            pl.inc("vector", lambda e, x2b=x2b, xb=xb: e.tensor_tensor(x2b, wb_sq, xb, AO.add), sV)
            pl.wait("gpsimd", sV)
            self.dma("gpsimd", self.x2_s[b, tp * 128:(tp + 1) * 128, :], x2b, "dX2" + fx)
            pl.inc_wait("vector", lambda e, x2b=x2b, mu=mu: e.reduce_sum(mu, x2b, AX.X))
            pl.wait("scalar", sV)
            pl.wait("scalar", "iw_vector")
            pl.inc("scalar", lambda e, x2b=x2b, ssq=ssq: e.activation(
                wb_sq, x2b, AF.Square, accum_out=ssq), sS)
            pl.inc_wait("vector", lambda e, mu=mu: e.tensor_scalar(mu, mu, 1.0 / D, None, AO.mult))
            pl.wait("vector", sS)
            pl.inc_wait("vector", lambda e, ssq=ssq: e.tensor_scalar(ssq, ssq, 1.0 / D, EPS, AO.mult, op1=AO.add))
            pl.inc_wait("vector", lambda e, rr=rr, mu=mu: e.tensor_tensor(rr, mu, mu, AO.mult))
            pl.inc("vector", lambda e, ssq=ssq, rr=rr: e.tensor_tensor(ssq, ssq, rr, AO.subtract), sV)
            pl.wait("scalar", sV)
            pl.inc_wait("scalar", lambda e, ssq=ssq: e.activation(ssq, ssq, AF.Ln))
            pl.inc("scalar", lambda e, ssq=ssq, rr=rr: e.activation(rr, ssq, AF.Exp, scale=-0.5), sS)
            pl.wait("vector", sS)
            pl.inc_wait("vector", lambda e, mb2=mb2, mu=mu, rr=rr: e.tensor_tensor(mb2, mu, rr, AO.mult))
            pl.inc("vector", lambda e, mb2=mb2: e.tensor_scalar(mb2, mb2, -1.0, None, AO.mult), sV)
            pl.wait("scalar", sV)
            if tp >= 2:
                pl.wait("scalar", "dH2" + fx)
            pl.inc("scalar", lambda e, h2b=h2b, x2b=x2b, rr=rr, mb2=mb2: e.activation(
                h2b, x2b, AF.Identity, scale=rr, bias=mb2), sS)
            pl.wait("gpsimd", sS)
            self.dma("gpsimd", self.h2_s[b, tp * 128:(tp + 1) * 128, :], h2b, "dH2" + fx)
            if self.debug and b == 0 and tp < 2:
                pl.wait("gpsimd", sV)
                pl.wait("gpsimd", sS)
                self.tap(f"p2_x2_tp{tp}", x2b)
                self.tap(f"p2_h2_tp{tp}", h2b)
            pl.inc("scalar", lambda e, h2b=h2b, ssq=ssq: e.activation(
                wb_sq, h2b, AF.Square, accum_out=ssq), sS)
            pl.wait("scalar", sS)
            pl.inc_wait("scalar", lambda e, ssq=ssq: e.activation(ssq, ssq, AF.Ln, scale=1.0 / D, bias=self.c_eps))
            pl.inc("scalar", lambda e, ssq=ssq, rr=rr: e.activation(rr, ssq, AF.Exp, scale=-0.5), sS)
            pl.wait("scalar", sS)
            pl.inc("scalar", lambda e, rmb=rmb, h2b=h2b, rr=rr: e.activation(
                rmb, h2b, AF.Identity, scale=rr, bias=0.0), sS)
            if STAGE < 87:
                continue
            pl.wait("tensor", sS)
            for c in range(8):
                pbt = self.bank(4 + c % 2)[:, 0:64].bitcast(BF16)
                if c >= 2 or tp > 0:
                    pl.wait("tensor", sS)
                pl.inc("tensor", lambda e, c=c, pbt=pbt, rmb=rmb: e.transpose(
                    pbt, rmb[:, c * 128:(c + 1) * 128], self.c_ident), sT)
                pl.wait("scalar", sT)
                pl.inc("scalar", lambda e, c=c, pbt=pbt, tp=tp: e.copy(
                    rms_fm[:, c * 2048 + tp * 128: c * 2048 + (tp + 1) * 128], pbt), sS)

        if STAGE < 88:
            return
        vhat = s(O, 16 * 16 * 65, BF16)
        wv_sb = s(O + 34 * 1024, 8 * 1024, BF16)
        qk_pool = s(O + 50 * 1024, 2 * 8 * 64, BF16)
        qstag = s(O + 52 * 1024, 2 * 512, BF16, rows=65)
        pl.wait("gpsimd", sT)
        pl.wait("gpsimd", sV)
        pl.wait("gpsimd", sS)
        self.dma("gpsimd", wv_sb.rearrange("p (f n) -> p f n", f=8),
                 self.w["wv"].rearrange("(f p) n -> p f n", p=128), "dG")
        pl.wait("tensor", "dG")
        v4 = vhat.rearrange("p (h k n) -> p h k n", h=16, k=16)
        for tp in range(16):
            pb = self.bank(2 * (tp % 2), 2)
            if tp >= 2:
                pl.wait("tensor", sS)
            for f in range(8):
                for pc in range(2):
                    pl.inc("tensor", lambda e, f=f, pc=pc, pb=pb, tp=tp: e.matmul(
                        pb[:, pc * 512:(pc + 1) * 512],
                        rms_fm[:, f * 2048 + tp * 128: f * 2048 + (tp + 1) * 128],
                        wv_sb.rearrange("p (f n) -> p f n", f=8)[:, f, pc * 512:(pc + 1) * 512],
                        start=(f == 0), stop=(f == 7)), sT)
            pl.wait("scalar", sT)
            pl.inc("scalar", lambda e, pb=pb, tp=tp: e.copy(
                v4[:, :, tp, 0:64], pb.rearrange("p (h n) -> p h n", h=16)), sS)
        pl.wait("gpsimd", sS)
        pl.inc("gpsimd", lambda e: e.memset(v4[:, :, :, 64:65], 1.0), "gV" + fx)
        khat = s(O + 104 * 1024, 16 * 2048, BF16)
        for mat, dst in (("wq", "Q"), ("wk", "K")):
            for h in range(NH):
                pool = qk_pool[:, (h % 2) * 512:(h % 2) * 512 + 512].rearrange(
                    "p (f n) -> p f n", f=8)
                if h >= 2:
                    pl.wait("gpsimd", sT)
                self.dma("gpsimd", pool,
                         self.w[mat].rearrange("(f p) n -> p f n", p=128)[:, :, h * 64:(h + 1) * 64],
                         "dQ" + fx)
                pl.wait("tensor", "dQ" + fx)
                pl.wait("tensor", sS)
                pl.wait("tensor", "sq" + fx)
                for tcq in range(4):
                    pbq = self.bank(4 + tcq)[0:64, :]
                    for f in range(8):
                        pl.inc("tensor", lambda e, f=f, tcq=tcq, pool=pool, pbq=pbq: e.matmul(
                            pbq, pool[:, f, :],
                            rms_fm[:, f * 2048 + tcq * 512: f * 2048 + (tcq + 1) * 512],
                            start=(f == 0), stop=(f == 7)), sT)
                pl.wait("scalar", sT)
                for tcq in range(4):
                    pbq = self.bank(4 + tcq)[0:64, :]
                    if dst == "Q":
                        qs = qstag[:, (tcq % 2) * 512:(tcq % 2 + 1) * 512]
                        if tcq >= 2:
                            pl.wait("scalar", "dQs" + fx)
                        pl.inc("scalar", lambda e, pbq=pbq, qs=qs: e.activation(
                            qs[0:64, :], pbq, AF.Identity, scale=0.125, bias=0.0), "sq" + fx)
                        pl.wait("gpsimd", "sq" + fx)
                        self.dma("gpsimd", self.qhat_s[b, h, 0:64, tcq * 512:(tcq + 1) * 512],
                                 qs[0:64, :], "dQs" + fx)
                    else:
                        pl.inc("scalar", lambda e, pbq=pbq, h=h, tcq=tcq: e.copy(
                            khat[0:64, h * 2048 + tcq * 512: h * 2048 + (tcq + 1) * 512], pbq), sS)
        pl.wait("gpsimd", sS)
        pl.inc("gpsimd", lambda e: e.memset(
            khat.rearrange("p (h n) -> p h n", h=16)[64:65, :, :], 1.0), "gV" + fx)
        if self.debug and b == 0:
            pl.wait("gpsimd", sS)
            self.tap("p2_rmsfm", rms_fm)
            self.tap("p2_khat", khat)
            self.tap("p2_vhat", vhat)

    def emit_P3(self, b):
        if STAGE < 90:
            return
        pl = self.pl
        s = self.sbv
        O = self.OFF_P
        fx = "3"
        sV, sS, sT = "v" + fx, "s" + fx, "t" + fx
        vhat = s(O, 16 * 16 * 65, BF16)
        v4 = vhat.rearrange("p (h k n) -> p h k n", h=16, k=16)
        khat = s(O + 104 * 1024, 16 * 2048, BF16)
        wo_sb = s(O + 34 * 1024, 8 * 1024, BF16)
        qsb = s(O + 50 * 1024, 2 * 512, BF16, rows=65)
        qsq = s(O + 52 * 1024, 512, BF16, rows=64)
        lnk_all = s(O + 53 * 1024, 16, F32)
        ptil = s(O + 54 * 1024, 2 * 1024, BF16)
        o_all = s(O + 58 * 1024, 8 * 512, BF16)
        h2sb = s(O + 66 * 1024, 2 * 1024, F32)
        tsb = s(O + 74 * 1024, 2 * 1024, F32)
        gtm = s(O + 82 * 1024, 2 * 1024, BF16)
        p3sq = s(O + 86 * 1024, 1024, F32)
        p3st = s(O + 90 * 1024, 16, F32)
        ksq = s(O + 91 * 1024, 2048, BF16, rows=64)
        rl_sb = s(O + 95 * 1024, 512, F32)
        lb_sb = s(O + 96 * 1024, 512, BF16, rows=128)
        rtm = s(O + 98 * 1024, 16, F32)
        pl.wait("gpsimd", "t2")
        pl.wait("gpsimd", "s2")
        self.dma("gpsimd", wo_sb.rearrange("p (c n) -> p c n", c=8),
                 self.w["wo"].rearrange("(c p) n -> p c n", p=128), "dG")
        pl.wait("sync", "dQs2")

        for h in range(NH):
            pl.inc("scalar", lambda e, h=h: e.activation(
                ksq, khat[0:64, h * 2048:(h + 1) * 2048], AF.Square), "sk" + fx)
            pl.wait("tensor", "sk" + fx)
            pkn = self.bank(4, 4)[0:1, :]
            if h > 0:
                pl.wait("tensor", sV)
            for pc in range(4):
                pl.inc("tensor", lambda e, pc=pc, pkn=pkn: e.matmul(
                    pkn[:, pc * 512:(pc + 1) * 512], self.c_onescol[0:64, :],
                    ksq[:, pc * 512:(pc + 1) * 512], start=True, stop=True), sT)
            pl.wait("vector", sT)
            pl.inc_wait("vector", lambda e, pkn=pkn: e.reduce_max(rtm[0:1, 0:1], pkn, AX.X))
            pl.inc("vector", lambda e, h=h: e.tensor_copy(rtm[0:1, h + 0:h + 1], rtm[0:1, 0:1]), sV) \
                if False else pl.inc("vector", lambda e, h=h: e.tensor_copy(
                    lnk_all[0:1, h:h + 1], rtm[0:1, 0:1]), sV)
        pl.wait("scalar", sV)
        pl.inc("scalar", lambda e: e.activation(
            lnk_all[0:1, :], lnk_all[0:1, :], AF.Ln), "sk" + fx)
        pl.wait("tensor", "sk" + fx)
        pkb = self.bank(5)[:, 0:16]
        pl.inc("tensor", lambda e, pkb=pkb: e.matmul(
            pkb, self.c_ones[0:1, :], lnk_all[0:1, :], start=True, stop=True), sT)
        pl.wait("scalar", sT)
        pl.inc_wait("scalar", lambda e, pkb=pkb: e.activation(
            lnk_all, pkb, AF.Identity, scale=0.5, bias=0.0))

        for qc in range(4):
            oc3 = o_all.rearrange("p (c n) -> p c n", c=8)
            for h in range(NH):
                par = (qc * NH + h) % 2
                qt = qsb[:, par * 512:(par + 1) * 512]
                if qc * NH + h >= 2:
                    pl.wait("sync", sT)
                self.dma("sync", qt[0:64, :], self.qhat_s[b, h, 0:64, qc * 512:(qc + 1) * 512],
                         "dQ" + fx)
                pl.wait("scalar", "dQ" + fx)
                pl.inc("scalar", lambda e, qt=qt: e.activation(qsq, qt[0:64, :], AF.Square), "sq" + fx)
                pl.wait("tensor", "sq" + fx)
                pM = self.bank(5)[64:65, 0:512]
                if qc * NH + h > 0:
                    pl.wait("tensor", "sm" + fx)
                pl.inc("tensor", lambda e, pM=pM: e.matmul(
                    pM, self.c_onescol[0:64, :], qsq, start=True, stop=True), sT)
                pl.wait("scalar", sT)
                pl.inc_wait("scalar", lambda e, pM=pM: e.activation(pM, pM, AF.Ln))
                pl.inc("scalar", lambda e, pM=pM, qt=qt, h=h: e.activation(
                    qt[64:65, :], pM, AF.Exp, scale=0.5,
                    bias=lnk_all[64:65, h:h + 1]), "sm" + fx)
                pl.wait("vector", "sm" + fx)
                pl.inc("vector", lambda e, qt=qt: e.tensor_scalar(
                    qt[64:65, :], qt[64:65, :], -1.0, None, AO.mult), "vm" + fx)
                pl.wait("tensor", "vm" + fx)
                if qc * NH + h > 0:
                    pl.wait("tensor", "so" + fx)
                pl.wait("tensor", sV)
                nkc = 4 * (qc + 1)
                m3 = self.c_mask01.rearrange("p (r n) -> p r n", r=4)
                pO = self.bank(4)[0:65, :]
                for kp in range(nkc // 2):
                    pg = kp % 2
                    if kp >= 2:
                        pl.wait("tensor", "vav" + fx)
                    for i in range(2):
                        kc = 2 * kp + i
                        pS = self.bank(2 * pg + i)
                        pl.inc("tensor", lambda e, pS=pS, h=h, kc=kc, qt=qt: e.matmul(
                            pS, khat[0:65, h * 2048 + kc * 128: h * 2048 + (kc + 1) * 128],
                            qt, start=True, stop=True), sT)
                    pl.wait("scalar", sT)
                    pl.inc("scalar", lambda e, pg=pg: e.activation(
                        ptil[:, pg * 1024:(pg + 1) * 1024],
                        self.bank(2 * pg, 2), AF.Exp), "se" + fx)
                    if 2 * kp >= 4 * qc:
                        pl.wait("vector", "se" + fx)
                        for i in range(2):
                            kc = 2 * kp + i
                            r = kc - 4 * qc
                            sl = ptil[:, pg * 1024 + i * 512: pg * 1024 + (i + 1) * 512]
                            pl.inc("vector", lambda e, sl=sl, r=r: e.tensor_tensor(
                                sl, sl, m3[:, r, :], AO.mult), "se" + fx)
                    pl.wait("tensor", "se" + fx)
                    for i in range(2):
                        kc = 2 * kp + i
                        pl.inc("tensor", lambda e, pO=pO, h=h, kc=kc, pg=pg, i=i: e.matmul(
                            pO, v4[:, h, kc, :],
                            ptil[:, pg * 1024 + i * 512: pg * 1024 + (i + 1) * 512],
                            start=(kc == 0), stop=(kc == nkc - 1)), "vav" + fx)
                pl.wait("vector", "vav" + fx)
                pl.inc("vector", lambda e, pO=pO: e.reciprocal(
                    rl_sb[64:65, :], pO[64:65, :]), "vr" + fx)
                pl.wait("tensor", "vr" + fx)
                pLb = self.bank(6)[0:64, :]
                pl.inc("tensor", lambda e, pLb=pLb: e.matmul(
                    pLb, self.c_ones[64:65, 0:64], rl_sb[64:65, :],
                    start=True, stop=True), sT)
                pl.wait("scalar", sT)
                pl.inc("scalar", lambda e, pLb=pLb: e.copy(lb_sb[0:64, :], pLb), "sl" + fx)
                pl.wait("vector", "sl" + fx)
                pl.inc("vector", lambda e, pO=pO, h=h, oc3=oc3: e.tensor_tensor(
                    oc3[(h % 2) * 64:(h % 2) * 64 + 64, h // 2, :],
                    pO[0:64, :], lb_sb[0:64, :], AO.mult), "so" + fx)
            if self.debug and b == 0 and qc == 0:
                pl.wait("gpsimd", "so" + fx)
                self.tap("p3_oall_qc0", o_all)
            pl.wait("tensor", "so" + fx)
            pl.wait("tensor", "dG")
            wo3 = wo_sb.rearrange("p (c n) -> p c n", c=8)
            for tp4 in range(4):
                tpg = qc * 4 + tp4
                par = tp4 % 2
                pb = self.bank(5 + par, 1)
                pb = self.bank(6 if par else 2, 2)
                h2b = h2sb[:, par * 1024:(par + 1) * 1024]
                tb = tsb[:, par * 1024:(par + 1) * 1024]
                gb = gtm[:, par * 1024:(par + 1) * 1024]
                if tpg >= 2:
                    pl.wait("gpsimd", sV)
                self.dma("gpsimd", h2b, self.h2_s[b, tpg * 128:(tpg + 1) * 128, :], "dH" + fx)
                if tp4 >= 2:
                    pl.wait("tensor", sV)
                for dc in range(8):
                    for pc in range(2):
                        pl.inc("tensor", lambda e, dc=dc, pc=pc, pb=pb, tp4=tp4, oc3=oc3: e.matmul(
                            pb[:, pc * 512:(pc + 1) * 512],
                            oc3[:, dc, tp4 * 128:(tp4 + 1) * 128],
                            wo3[:, dc, pc * 512:(pc + 1) * 512],
                            start=(dc == 0), stop=(dc == 7)), sT)
                pl.wait("vector", sT)
                pl.wait("vector", "dH" + fx)
                if tpg >= 2:
                    pl.wait("vector", "dT" + fx)
                pl.inc("vector", lambda e, tb=tb, pb=pb, h2b=h2b: e.tensor_tensor(
                    tb, pb, h2b, AO.add), sV)
                pl.wait("gpsimd", sV)
                self.dma("gpsimd", self.t_s[b, tpg * 128:(tpg + 1) * 128, :], tb, "dT" + fx)
                pl.wait("scalar", sV)
                ssq = p3st[:, 0:1]; rr = p3st[:, 1:2]
                pl.inc("scalar", lambda e, tb=tb, ssq=ssq: e.activation(
                    p3sq, tb, AF.Square, accum_out=ssq), "s3" + fx)
                pl.wait("scalar", "s3" + fx)
                pl.inc_wait("scalar", lambda e, ssq=ssq: e.activation(
                    ssq, ssq, AF.Ln, scale=1.0 / D, bias=self.c_eps))
                pl.inc_wait("scalar", lambda e, ssq=ssq, rr=rr: e.activation(
                    rr, ssq, AF.Exp, scale=-0.5))
                if tpg >= 2:
                    pl.wait("scalar", "dGs" + fx)
                pl.inc("scalar", lambda e, gb=gb, tb=tb, rr=rr: e.activation(
                    gb, tb, AF.Identity, scale=rr, bias=0.0), "s3" + fx)
                pl.wait("gpsimd", "s3" + fx)
                self.dma("gpsimd", self.g_s[b, tpg * 128:(tpg + 1) * 128, :], gb, "dGs" + fx)
                if self.debug and b == 0 and tpg < 2:
                    pl.wait("gpsimd", sV)
                    self.tap(f"p3_t_tp{tpg}", tb)
                    self.tap(f"p3_g_tp{tpg}", gb)

    def emit_P4(self, b):
        if STAGE < 100:
            return
        pl = self.pl
        s = self.sbv
        O = self.OFF_P
        fx = "4"
        sV, sS, sT = "v" + fx, "s" + fx, "t" + fx
        fc1sb = s(O, 8 * 2048, BF16)
        fc2sb = s(O + 32 * 1024, 16 * 1024, BF16)
        h1act = s(O + 64 * 1024, 16 * 1024, BF16)
        g_fm = s(O + 96 * 1024, 8 * 2048, BF16)
        acc = s(O + 128 * 1024, 8 * 1024, F32)
        crep = s(O + 160 * 1024, 1024, BF16)
        c_fm = s(O + 162 * 1024, 2048, BF16, rows=8)
        gld = s(O + 166 * 1024, 2 * 1024, BF16)
        tld = s(O + 170 * 1024, 2 * 1024, F32)
        x2ld = s(O + 178 * 1024, 2 * 1024, F32)
        p4sq = s(O + 186 * 1024, 1024, F32)
        p4st = s(O + 190 * 1024, 16, F32)
        cstg = s(O + 191 * 1024, 1024, BF16, rows=1)
        pl.wait("sync", "dWc")
        pl.wait("gpsimd", "dWc")
        pl.wait("gpsimd", "t3")
        pl.wait("gpsimd", "s3")
        pl.wait("gpsimd", "v3")
        pl.wait("sync", "t3")
        pl.wait("sync", "s3")
        pl.wait("sync", "dT3")
        pl.wait("sync", "dX22")
        sR = "sr" + fx
        for tp in range(16):
            par = tp % 2
            gb = gld[:, par * 1024:(par + 1) * 1024]
            if tp >= 2:
                pl.wait("gpsimd", sT)
            self.dma("gpsimd", gb, self.g_s[b, tp * 128:(tp + 1) * 128, :], "dGl" + fx)
            pl.wait("tensor", "dGl" + fx)
            for c in range(8):
                pbt = self.bank(4)[:, 64 * (c % 2):64 * (c % 2) + 64].bitcast(BF16)
                if c >= 2 or tp > 0:
                    pl.wait("tensor", sS)
                pl.inc("tensor", lambda e, c=c, pbt=pbt, gb=gb: e.transpose(
                    pbt, gb[:, c * 128:(c + 1) * 128], self.c_ident), sT)
                pl.wait("scalar", sT)
                pl.inc("scalar", lambda e, c=c, pbt=pbt, tp=tp: e.copy(
                    g_fm[:, c * 2048 + tp * 128: c * 2048 + (tp + 1) * 128], pbt), sS)
        g3 = g_fm.rearrange("p (c n) -> p c n", c=8)
        r3 = self.c_router.rearrange("p (c n) -> p c n", c=8)
        rtm = p4st
        pl.wait("tensor", sS)
        for tp in range(16):
            pR = self.bank(7)[:, 0:8]
            if tp > 0:
                pl.wait("tensor", "vR" + fx)
                pl.wait("tensor", sS)
            for f in range(8):
                pl.inc("tensor", lambda e, f=f, tp=tp, pR=pR: e.matmul(
                    pR, g3[:, f, tp * 128:(tp + 1) * 128], r3[:, f, :],
                    start=(f == 0), stop=(f == 7)), sT)
            pl.wait("vector", sT)
            lg = rtm[:, 0:8]
            m1 = rtm[:, 8:9]; m2 = rtm[:, 9:10]; dd = rtm[:, 10:11]
            den = rtm[:, 11:12]
            sc8 = p4sq[:, 0:8]
            e8 = p4sq[:, 8:16]
            cb8 = p4sq[:, 16:24].bitcast(BF16)[:, 0:8]
            pl.inc_wait("vector", lambda e, pR=pR, lg=lg: e.tensor_copy(lg, pR))
            pl.inc_wait("vector", lambda e, lg=lg, m1=m1: e.reduce_max(m1, lg, AX.X))
            pl.inc("vector", lambda e, sc8=sc8, lg=lg, m1=m1: e.tensor_scalar(
                sc8, lg, m1, None, AO.subtract), "vR" + fx)
            pl.wait("scalar", "vR" + fx)
            pl.inc("scalar", lambda e, sc8=sc8: e.activation(sc8, sc8, AF.Sign), "sR" + fx)
            pl.wait("vector", "sR" + fx)
            pl.inc_wait("vector", lambda e, sc8=sc8: e.tensor_scalar(sc8, sc8, 1.0, None, AO.add))
            pl.inc_wait("vector", lambda e, sc8=sc8, lg=lg: e.scalar_tensor_tensor(
                sc8, sc8, -1e30, lg, AO.mult, AO.add))
            pl.inc_wait("vector", lambda e, sc8=sc8, m2=m2: e.reduce_max(m2, sc8, AX.X))
            pl.inc("vector", lambda e, dd=dd, m2=m2, m1=m1: e.tensor_tensor(
                dd, m2, m1, AO.subtract), "vR" + fx)
            pl.wait("scalar", "vR" + fx)
            pl.inc("scalar", lambda e, dd=dd, den=den: e.activation(den, dd, AF.Exp), "sR" + fx)
            pl.wait("vector", "sR" + fx)
            pl.inc_wait("vector", lambda e, den=den: e.tensor_scalar(den, den, 1.0, None, AO.add))
            pl.inc_wait("vector", lambda e, den=den: e.reciprocal(den, den))
            pl.inc("vector", lambda e, sc8=sc8, lg=lg, m2=m2: e.tensor_scalar(
                sc8, lg, m2, None, AO.subtract), "vR" + fx)
            pl.wait("scalar", "vR" + fx)
            pl.inc("scalar", lambda e, sc8=sc8: e.activation(sc8, sc8, AF.Sign), "sR" + fx)
            pl.wait("vector", "sR" + fx)
            pl.inc_wait("vector", lambda e, sc8=sc8: e.tensor_scalar(
                sc8, sc8, 1.0, 1.0, AO.add, op1=AO.min))
            pl.inc("vector", lambda e, lg=lg, m1=m1, e8=e8: e.tensor_scalar(
                e8, lg, m1, None, AO.subtract), "vR" + fx)
            pl.wait("scalar", "vR" + fx)
            pl.inc("scalar", lambda e, e8=e8: e.activation(e8, e8, AF.Exp), "sR" + fx)
            pl.wait("vector", "sR" + fx)
            pl.inc_wait("vector", lambda e, e8=e8, sc8=sc8: e.tensor_tensor(e8, e8, sc8, AO.mult))
            pl.inc("vector", lambda e, e8=e8, den=den, cb8=cb8: e.tensor_scalar(
                cb8, e8, den, None, AO.mult), "vR" + fx)
            pl.wait("tensor", "vR" + fx)
            pc8 = self.bank(6)[0:8, 0:64].bitcast(BF16)[:, 0:128]
            pl.inc("tensor", lambda e, cb8=cb8, pc8=pc8: e.transpose(
                pc8, cb8, self.c_ident), sT)
            pl.wait("scalar", sT)
            pl.inc("scalar", lambda e, pc8=pc8, tp=tp: e.copy(
                c_fm[0:8, tp * 128:(tp + 1) * 128], pc8), "sR" + fx)
        f13 = fc1sb.rearrange("p (f n) -> p f n", f=8)
        f23 = fc2sb.rearrange("p (c n) -> p c n", c=16)
        h13 = h1act.rearrange("p (c n) -> p c n", c=16)
        a3 = acc.rearrange("p (c n) -> p c n", c=8)
        if self.debug and b == 0:
            pl.wait("gpsimd", "sR" + fx)
            self.tap("p4_gfm", g_fm)
            self.tap("p4_cfm", c_fm)
        for grp in range(2):
            gcol = grp * 1024
            pl.wait("vector", sT)
            pl.inc("vector", lambda e: e.memset(acc, 0.0), sV)
            for e_i in range(NE):
                pl.wait("gpsimd", sT)
                self.dma("gpsimd", f13,
                         self.w["exp_fc1"].rearrange("e (c p) n -> e p c n", p=128)[e_i], "dW" + fx)
                self.dma("gpsimd", f23,
                         self.w["exp_fc2"].rearrange("e (c p) n -> e p c n", p=128)[e_i], "dW" + fx)
                
                pl.wait("tensor", "s3")
                if e_i > 0 or grp > 0:
                    pl.wait("gpsimd", "vh" + fx)
                    pl.wait("gpsimd", "tc" + fx)
                self.dma("gpsimd", cstg, c_fm[e_i:e_i + 1, gcol:gcol + 1024], "dC" + fx)
                pl.wait("tensor", "dC" + fx)
                pcb = self.bank(6, 2)
                if e_i > 0 or grp > 0:
                    pl.wait("tensor", "sc" + fx)
                for pc in range(2):
                    pl.inc("tensor", lambda e, pc=pc, pcb=pcb: e.matmul(
                        pcb[:, pc * 512:(pc + 1) * 512], self.c_onesb[0:1, :],
                        cstg[0:1, pc * 512:(pc + 1) * 512], start=True, stop=True), "tc" + fx)
                pl.wait("scalar", "tc" + fx)
                pl.inc("scalar", lambda e, pcb=pcb: e.copy(crep, pcb), "sc" + fx)
                pl.wait("tensor", "dW" + fx)
                for hp in range(16):
                    pg = hp % 2
                    pH = self.bank(2 * pg, 2)
                    if hp >= 2:
                        pl.wait("tensor", "sh" + fx)
                    for f in range(8):
                        for tc2 in range(2):
                            pl.inc("tensor", lambda e, f=f, hp=hp, tc2=tc2, pH=pH, gcol=gcol: e.matmul(
                                pH[:, tc2 * 512:(tc2 + 1) * 512],
                                f13[:, f, hp * 128:(hp + 1) * 128],
                                g3[:, f, gcol + tc2 * 512: gcol + (tc2 + 1) * 512],
                                start=(f == 0), stop=(f == 7)), sT)
                    pl.wait("scalar", sT)
                    pl.inc("scalar", lambda e, hp=hp, pH=pH: e.activation(
                        h13[:, hp, :], pH, AF.Relu), "sh" + fx)
                    pl.wait("vector", "sh" + fx)
                    pl.inc("vector", lambda e, hp=hp: e.tensor_tensor(
                        h13[:, hp, :], h13[:, hp, :], h13[:, hp, :], AO.mult), "vh" + fx)
                    pl.wait("vector", "sc" + fx)
                    pl.inc("vector", lambda e, hp=hp: e.tensor_tensor(
                        h13[:, hp, :], h13[:, hp, :], crep, AO.mult), "vh" + fx)
                pl.wait("tensor", "vh" + fx)
                for dp in range(8):
                    pg = dp % 2
                    pF = self.bank(2 * pg, 2)
                    if dp >= 2 or True:
                        pl.wait("tensor", sV)
                    for hc in range(16):
                        for tc2 in range(2):
                            pl.inc("tensor", lambda e, dp=dp, hc=hc, tc2=tc2, pF=pF: e.matmul(
                                pF[:, tc2 * 512:(tc2 + 1) * 512],
                                f23[:, hc, dp * 128:(dp + 1) * 128],
                                h13[:, hc, tc2 * 512:(tc2 + 1) * 512],
                                start=(hc == 0), stop=(hc == 15)), sT)
                    pl.wait("vector", sT)
                    pl.inc("vector", lambda e, dp=dp, pF=pF: e.tensor_tensor(
                        a3[:, dp, :], a3[:, dp, :], pF, AO.add), sV)
            if self.debug and b == 0 and grp == 0:
                pl.wait("gpsimd", sV)
                pl.wait("gpsimd", "vh" + fx)
                self.tap("p4_h1act_e7", h1act)
                self.tap("p4_acc_g0", acc)
                self.tap("p4_fc1sb_e7", fc1sb)
                self.tap("p4_fc2sb_e7", fc2sb)
                self.tap("p4_crep_e7", crep)
            for tp4 in range(8):
                tpg = grp * 8 + tp4
                par = tp4 % 2
                tb = tld[:, par * 1024:(par + 1) * 1024]
                x2b = x2ld[:, par * 1024:(par + 1) * 1024]
                if tp4 >= 2 or grp > 0:
                    pl.wait("sync", sV)
                    pl.wait("sync", "dO")
                self.dma("sync", tb, self.t_s[b, tpg * 128:(tpg + 1) * 128, :], "dF" + fx)
                self.dma("sync", x2b, self.x2_s[b, tpg * 128:(tpg + 1) * 128, :], "dF" + fx)
                pl.wait("tensor", sV)
                pT = self.bank(2, 2)
                for dc in range(8):
                    pl.inc("tensor", lambda e, dc=dc, tp4=tp4, pT=pT: e.transpose(
                        pT[:, dc * 128:(dc + 1) * 128],
                        a3[:, dc, tp4 * 128:(tp4 + 1) * 128], self.c_identf), sT)
                pl.wait("vector", sT)
                pl.wait("vector", "dF" + fx)
                pl.inc("vector", lambda e, tb=tb, pT=pT: e.tensor_tensor(
                    tb, tb, pT, AO.add), sV)
                pl.wait("vector", sV)
                mu = p4st[:, 0:1]; ssq = p4st[:, 1:2]; rr = p4st[:, 2:3]; mb2 = p4st[:, 3:4]
                pl.inc_wait("vector", lambda e, mu=mu, tb=tb: e.reduce_sum(mu, tb, AX.X))
                pl.wait("scalar", sV)
                pl.inc("scalar", lambda e, ssq=ssq, tb=tb: e.activation(
                    p4sq, tb, AF.Square, accum_out=ssq), sS)
                pl.inc_wait("vector", lambda e, mu=mu: e.tensor_scalar(mu, mu, 1.0 / D, None, AO.mult))
                pl.wait("vector", sS)
                pl.inc_wait("vector", lambda e, ssq=ssq: e.tensor_scalar(ssq, ssq, 1.0 / D, EPS, AO.mult, op1=AO.add))
                pl.inc_wait("vector", lambda e, rr=rr, mu=mu: e.tensor_tensor(rr, mu, mu, AO.mult))
                pl.inc("vector", lambda e, ssq=ssq, rr=rr: e.tensor_tensor(ssq, ssq, rr, AO.subtract), sV)
                pl.wait("scalar", sV)
                pl.inc_wait("scalar", lambda e, ssq=ssq: e.activation(ssq, ssq, AF.Ln))
                pl.inc("scalar", lambda e, ssq=ssq, rr=rr: e.activation(rr, ssq, AF.Exp, scale=-0.5), sS)
                pl.wait("vector", sS)
                pl.inc_wait("vector", lambda e, mb2=mb2, mu=mu, rr=rr: e.tensor_tensor(mb2, mu, rr, AO.mult))
                pl.inc("vector", lambda e, mb2=mb2: e.tensor_scalar(mb2, mb2, -1.0, None, AO.mult), sV)
                pl.wait("scalar", sV)
                pl.inc("scalar", lambda e, rr=rr, mb2=mb2, tb=tb: e.activation(
                    p4sq, tb, AF.Identity, scale=rr, bias=mb2), sS)
                pl.wait("vector", sS)
                if tp4 >= 2:
                    pl.wait("vector", "dO")
                pl.inc("vector", lambda e, x2b=x2b: e.tensor_tensor(
                    x2b, p4sq, x2b, AO.add), sV)
                pl.wait("gpsimd", sV)
                self.dma("gpsimd", self.y_out[b, tpg * 128:(tpg + 1) * 128, :], x2b, "dO")

    def emit_final(self):
        pl = self.pl
        if STAGE < 100:
            z = self.sbv(self.OFF_P, 1024, F32)
            pl.wait("sync", "dS")
            for b in range(BPC):
                for tt in range(L // 128):
                    self.dma("sync", z, self.x_in[b, tt * 128:(tt + 1) * 128, :], "dF")
                    pl.wait("sync", "dF")
                    self.dma("sync", self.y_out[b, tt * 128:(tt + 1) * 128, :], z, "dO")
        pl.wait("gpsimd", "dO")
        pl.wait("sync", "dWc")
        pl.wait("gpsimd", "dS")
        if "dDBG" in pl.cnt:
            pl.wait("gpsimd", "dDBG")


_cache = {}


def build_program(debug=False):
    key = "dbg" if debug else "plain"
    if key not in _cache:
        b = Builder(debug=debug)
        _cache[key] = (b.build(), b.dbg_outs)
    return _cache[key]


WMAP = {
    "W_in": "L0_attn_W_in", "conv_w": "L0_attn_conv_w", "conv_b": "L0_attn_conv_b",
    "W_xproj": "L0_attn_W_xproj", "W_dt": "L0_attn_W_dt", "b_dt": "L0_attn_b_dt",
    "A_log": "L0_attn_A_log", "Dp": "L0_attn_Dp", "W_out": "L0_attn_W_out",
    "wq": "L0_ffn_wq", "wk": "L0_ffn_wk", "wv": "L0_ffn_wv", "wo": "L0_ffn_wo",
    "router_w": "L0_ffn_router_w", "exp_fc1": "L0_ffn_exp_fc1",
    "exp_fc2": "L0_ffn_exp_fc2",
}


def run(inputs, debug=False):
    x = np.ascontiguousarray(inputs["L0_attn_x"])
    nc, dbg = build_program(debug=debug)
    in_maps = []
    for c in range(8):
        m = {"x": x[c * BPC:(c + 1) * BPC]}
        for k, v in WMAP.items():
            m[k] = np.ascontiguousarray(inputs[v])
        in_maps.append(m)
    res = run_bass_kernel_spmd(nc, in_maps, core_ids=list(range(8)))
    return res


def kernel(**inputs):
    res = run(inputs)
    return np.concatenate([np.asarray(r["y"]) for r in res.results], axis=0)
